# version 16
# baseline (speedup 1.0000x reference)
import math
import numpy as np
import ml_dtypes
import concourse.bass as bass
import concourse.mybir as mybir
from concourse.bass_utils import run_bass_kernel_spmd
from concourse.bass import ds
from contextlib import ExitStack

F32 = mybir.dt.float32
BF16 = mybir.dt.bfloat16
I8 = mybir.dt.int8
I32 = mybir.dt.int32
U8 = mybir.dt.uint8
AF = mybir.ActivationFunctionType
ALU = mybir.AluOpType
AX = mybir.AxisListType
ET = mybir.EngineType

NC = 8
WAVE = 4
H = 32
HKV = 8
HD = 128
GS = 4
HDTOT = H * HD
KVD = HKV * HD
SQD = math.sqrt(HD)
INV = 1.0 / SQD
EPS_RMS = 1e-6
EPS_LN = 1e-5
SHIFT = 45.0
T_WIN = 60.0


class CFG:
    def __init__(self, S=8192, DM=4096):
        self.S = S
        self.DM = DM
        self.TOK = S // NC
        self.NT = S // 128
        self.NTO = self.TOK // 128
        self.DMC = DM // 128
        self.NSUB = S // 128
        self.RDIM = DM // 8
        self.RC = self.RDIM // 128
        self.DC = DM // 512
        self.HC = HDTOT // 512
        self.KC = KVD // 512
        assert self.RDIM % 128 == 0 and DM % 512 == 0 and self.TOK % 128 == 0


def slope_of(h):
    return 2.0 ** (-0.25 * (h + 1))


def chunk_range(cfg, g, T):
    c_hi = (128 * T + 127) // 512
    wmax = T_WIN / slope_of(8 * g + 7)
    c_lo = max(0, math.ceil((128 * T - 384 - wmax) / 512.0))
    return min(c_lo, c_hi), c_hi


def make_jobs(cfg):
    jobs = []
    for g in range(GS):
        for T in range(cfg.NT):
            c_lo, c_hi = chunk_range(cfg, g, T)
            for ci in range(c_lo, c_hi + 1):
                jobs.append((g, T, ci))
    return jobs


class Rec:
    ENG = ("t", "v", "s", "p", "y")

    def __init__(self):
        self.ops = {e: [] for e in self.ENG}
        self.cnt = {}
        self.lastwait = {e: {} for e in self.ENG}

    def inc(self, sem, n=1):
        self.cnt[sem] = self.cnt.get(sem, 0) + n
        return self.cnt[sem]

    def val(self, sem):
        return self.cnt.get(sem, 0)

    def wait(self, eng, sem, v):
        if v is None or v <= 0:
            return
        if self.lastwait[eng].get(sem, 0) >= v:
            return
        self.lastwait[eng][sem] = v
        self.ops[eng].append(("w", sem, v))

    def emit(self, eng, fn, sem=None, n=1):
        if sem in ("V", "S", "P"):
            self.wait(eng, sem, self.val(sem))
        if sem is None:
            self.ops[eng].append(("o", fn, None, 0))
            return None
        v = self.inc(sem, n)
        self.ops[eng].append(("o", fn, sem, n))
        return v


def _replay(rec, key, eng, sems):
    for item in rec.ops[key]:
        if item[0] == "w":
            eng.wait_ge(sems[item[1]], item[2])
        else:
            inst = item[1](eng)
            if item[2] is not None:
                inst.then_inc(sems[item[2]], item[3])


def build_program(cfg, upto=9):
    nc = bass.Bass("TRN2", target_bir_lowering=False, debug=False)
    S, DM, TOK = cfg.S, cfg.DM, cfg.TOK
    NT, NTO, DMC, NSUB = cfg.NT, cfg.NTO, cfg.DMC, cfg.NSUB
    RDIM, RC, DC = cfg.RDIM, cfg.RC, cfg.DC
    jobs = make_jobs(cfg)
    NJOBS = len(jobs)

    xa = nc.dram_tensor("x", [TOK, DM], F32, kind="ExternalInput").ap()
    xta = nc.dram_tensor("xt", [128, DMC * TOK], BF16, kind="ExternalInput").ap()
    _CW = (DM // 128) * 512
    wqt = nc.dram_tensor("wqt", [128, (HDTOT // 512) * _CW], BF16, kind="ExternalInput").ap()
    wkt = nc.dram_tensor("wkt", [128, (KVD // 512) * _CW], BF16, kind="ExternalInput").ap()
    wvt = nc.dram_tensor("wvt", [128, (KVD // 512) * _CW], BF16, kind="ExternalInput").ap()
    wot = nc.dram_tensor("wot", [128, (DM // 512) * H * 512], BF16, kind="ExternalInput").ap()
    kfh = nc.dram_tensor("kfh", [128, RC * DM], I8, kind="ExternalInput").ap()
    qnw = nc.dram_tensor("qnw", [1, HD], F32, kind="ExternalInput").ap()
    knw = nc.dram_tensor("knw", [1, HD], F32, kind="ExternalInput").ap()
    onesf = nc.dram_tensor("onesf", [1, 128], F32, kind="ExternalInput").ap()
    constv = nc.dram_tensor("constv", [128, 6], F32, kind="ExternalInput").ap()
    identb = nc.dram_tensor("identb", [128, 128], BF16, kind="ExternalInput").ap()
    identf = nc.dram_tensor("identf", [128, 128], F32, kind="ExternalInput").ap()
    lhs3 = nc.dram_tensor("lhs3", [3, GS * 128], BF16, kind="ExternalInput").ap()
    jvec2 = nc.dram_tensor("jvec2", [2, 512], BF16, kind="ExternalInput").ap()
    qib = nc.dram_tensor("qib", [1, GS * 512], F32, kind="ExternalInput").ap()
    selq = nc.dram_tensor("selq", [128, 512], BF16, kind="ExternalInput").ap()
    biastab = nc.dram_tensor("biastab", [128, max(NJOBS, 1)], F32, kind="ExternalInput").ap()
    maskst = nc.dram_tensor("maskst", [128, 4 * 512], U8, kind="ExternalInput").ap()
    dynoff = nc.dram_tensor("dynoff", [1, 16], I32, kind="ExternalInput").ap()
    yout = nc.dram_tensor("yout", [TOK, DM], F32, kind="ExternalOutput").ap()

    kag_i = nc.dram_tensor("kag_i", [KVD, TOK], BF16)
    vag_i = nc.dram_tensor("vag_i", [TOK, KVD], BF16)
    qag_i = nc.dram_tensor("qag_i", [HDTOT, TOK], BF16)
    oag_i = nc.dram_tensor("oag_i", [GS * 128, S], BF16)
    kag_o = nc.dram_tensor("kag_o", [NC * KVD + 512, TOK], BF16, addr_space="Shared")
    vag_o = nc.dram_tensor("vag_o", [NC * TOK, KVD], BF16, addr_space="Shared")
    qag_o = nc.dram_tensor("qag_o", [NC * HDTOT + 512, TOK], BF16, addr_space="Shared")
    oag_o = nc.dram_tensor("oag_o", [GS * NC * 128, S], BF16, addr_space="Shared")
    attd = nc.dram_tensor("attd", [TOK, DM], F32)
    lnscr = nc.dram_tensor("lnscr", [TOK, DM], BF16)
    ktok = nc.dram_tensor("ktok", [TOK, KVD], BF16)
    qtok = nc.dram_tensor("qtok", [TOK, 1024], BF16)
    oscr = nc.dram_tensor("oscr", [2 * 128, 128], BF16)
    oag_l = nc.dram_tensor("oag_l", [GS * NC * 128, TOK], BF16)

    rec = Rec()
    st = {}
    dyn = {}

    def dma(q, fn, sem=None):
        if sem is None:
            sem = "dG" if q == "p" else "dS"
        return rec.emit(q, fn, sem, 16)

    ALLSEMS = ["dG", "dS", "CC", "T", "V", "S", "P",
               "dW0", "dW1", "dW2", "dW3", "dX0", "dX1", "dA0", "dA1", "dT0", "dT1",
               "dP0", "dP1", "dY0", "dY1"]

    def phase_barrier():
        for eng in ("t", "v", "s", "p", "y"):
            for sem in ALLSEMS:
                if sem != "CC":
                    rec.wait(eng, sem, rec.val(sem))

    with ExitStack() as stk:
        e = stk.enter_context
        sems = {nm: e(nc.semaphore(nm)) for nm in
                ["dG", "dS", "CC", "T", "V", "S", "P",
                 "dW0", "dW1", "dW2", "dW3", "dX0", "dX1", "dA0", "dA1", "dT0", "dT1",
                 "dP0", "dP1", "dY0", "dY1"]}

        arena = e(nc.sbuf_tensor("arena", [128, 164 * 1024], U8))
        _ESZ = {F32: 4, BF16: 2, U8: 1, I32: 4}

        def carve(off_kb, shape, dtype):
            off = int(off_kb * 1024)
            n = shape[1] * _ESZ[dtype]
            assert off % 512 == 0 and off + n <= 164 * 1024, (off, n)
            v = arena[:, off:off + n].bitcast(dtype)
            return v

        onesf_s = e(nc.sbuf_tensor("onesf_s", [1, 128], F32))
        identb_s = e(nc.sbuf_tensor("identb_s", [128, 128], BF16))
        identf_s = e(nc.sbuf_tensor("identf_s", [128, 128], F32))
        onesb = e(nc.sbuf_tensor("onesb", [1, 128], BF16))
        onescol = e(nc.sbuf_tensor("onescol", [128, 1], BF16))
        negmub = e(nc.sbuf_tensor("negmub", [128, 1], BF16))
        nmrow = e(nc.sbuf_tensor("nmrow", [1, NTO * 128], BF16))
        rstd5 = e(nc.sbuf_tensor("rstd5", [128, NTO], F32))
        cv = e(nc.sbuf_tensor("cv", [128, 6], F32))
        lhs3_s = e(nc.sbuf_tensor("lhs3_s", [3, GS * 128], BF16))
        qib_s = e(nc.sbuf_tensor("qib_s", [1, GS * 512], F32))
        btab_s = e(nc.sbuf_tensor("btab_s", [128, max(NJOBS, 1)], F32))
        masks_s = e(nc.sbuf_tensor("masks_s", [128, 4 * 512], U8))
        dyn_s = e(nc.sbuf_tensor("dyn_s", [1, 16], I32))
        qnw_s = e(nc.sbuf_tensor("qnw_s", [1, HD], F32))
        knw_s = e(nc.sbuf_tensor("knw_s", [1, HD], F32))
        wqk_s = e(nc.sbuf_tensor("wqk_s", [128, HD], F32))
        zero_b = e(nc.sbuf_tensor("zero_b", [128, 512], BF16))
        neg_f = e(nc.sbuf_tensor("neg_f", [128, 512], F32))
        stag = [e(nc.sbuf_tensor(f"stag{i}", [3, 512], BF16)) for i in range(2)]
        mrow = [e(nc.sbuf_tensor(f"mrow{i}", [1, 128], F32)) for i in range(2)]
        abuf = [e(nc.sbuf_tensor(f"abuf{i}", [128, 512], BF16)) for i in range(4)]
        rcp_s = e(nc.sbuf_tensor("rcp_s", [128, 1], F32))
        outh_b = e(nc.sbuf_tensor("outh_b", [128, 128], BF16))
        ln_sum = [e(nc.sbuf_tensor(f"ln_sum{i}", [128, 1], F32)) for i in range(2)]
        ln_ssq = [e(nc.sbuf_tensor(f"ln_ssq{i}", [128, 1], F32)) for i in range(2)]
        ln_mean = [e(nc.sbuf_tensor(f"ln_mean{i}", [128, 1], F32)) for i in range(2)]
        ln_var = [e(nc.sbuf_tensor(f"ln_var{i}", [128, 1], F32)) for i in range(2)]
        ln_rstd = [e(nc.sbuf_tensor(f"ln_rstd{i}", [128, 1], F32)) for i in range(2)]
        ln_t1 = [e(nc.sbuf_tensor(f"ln_t1{i}", [128, 1], F32)) for i in range(2)]
        SQW = max(DM // 2, 2048)
        sq_scr = e(nc.sbuf_tensor("sq_scr", [128, SQW], F32))
        ln_h2 = [e(nc.sbuf_tensor(f"ln_h2{i}", [128, 1], F32)) for i in range(2)]
        rms_ssq = [e(nc.sbuf_tensor(f"rms_ssq{i}", [128, 4], F32)) for i in range(4)]
        rms_rstd = [e(nc.sbuf_tensor(f"rms_rstd{i}", [128, 4], F32)) for i in range(4)]

        dma("p", lambda g: g.dma_start(out=onesf_s[:, :], in_=onesf))
        dma("p", lambda g: g.dma_start(out=identb_s[:, :], in_=identb))
        dma("p", lambda g: g.dma_start(out=identf_s[:, :], in_=identf))
        dma("p", lambda g: g.dma_start(out=cv[:, :], in_=constv))
        dma("p", lambda g: g.dma_start(out=lhs3_s[:, :], in_=lhs3))
        dma("p", lambda g: g.dma_start(out=qib_s[:, :], in_=qib))
        dma("p", lambda g: g.dma_start(out=btab_s[:, :], in_=biastab))
        dma("p", lambda g: g.dma_start(out=masks_s[:, :], in_=maskst))
        dma("p", lambda g: g.dma_start(out=dyn_s[:, :], in_=dynoff))
        dma("p", lambda g: g.dma_start(out=qnw_s[:, :], in_=qnw))
        dma("p", lambda g: g.dma_start(out=knw_s[:, :], in_=knw))
        for i in range(2):
            dma("p", lambda g, i=i: g.dma_start(out=stag[i][1:3, :], in_=jvec2))
        ld0 = rec.val("dG")

        rec.wait("p", "dG", ld0)

        def load_dyn(g):
            mx = {"q": NC * HDTOT - NC * 512 + 512, "k": NC * KVD - NC * 512 + 512,
                  "o": (NC - 1) * TOK, "v": KVD - 128}
            for i, nm in enumerate(["q0", "q1", "q2", "q3", "k0", "k1", "k2", "k3", "oo",
                                    "v0", "v1", "v2", "v3"]):
                dyn[nm] = nc.values_load(dyn_s[0:1, i:i + 1], engines=[ET.Pool],
                                         min_val=0, max_val=mx[nm[0]],
                                         skip_runtime_bounds_check=True)
            return g.memset(rcp_s[:, :], 0.0)
        rec.emit("p", load_dyn, "P")
        rec.wait("y", "dG", ld0)

        def load_dyn_sp(y):
            mxq = NC * HDTOT - NC * 512 + 512
            for i in range(4):
                dyn["q%d_sp" % i] = nc.values_load(dyn_s[0:1, i:i + 1], engines=[ET.SP],
                                                   min_val=0, max_val=mxq,
                                                   skip_runtime_bounds_check=True)
            return None
        rec.emit("y", load_dyn_sp, None)

        PB = [e(nc.psum_tensor(f"PB{i}", [128, 512], F32)) for i in range(8)]
        pST_pre = PB[2:4]
        PBb = [PB[6].bitcast(BF16), PB[7].bitcast(BF16)]

        lnxT = carve(0, [128, DMC * TOK], BF16)
        kvsl = carve(64, [128, NTO * KVD], BF16)
        qnsl = carve(80, [128, 2 * NTO * 512], BF16)
        ktmp = [carve(96, [128, 512], BF16), carve(97, [128, 512], BF16),
                carve(162, [128, 512], BF16), carve(163, [128, 512], BF16)]
        p01 = PB[0:6]
        p01b = PBb

        rec.wait("v", "dG", ld0)
        rec.emit("v", lambda v: v.memset(zero_b[:, :], 0.0), "V")
        rec.emit("v", lambda v: v.memset(neg_f[:, :], -1.0e6), "V")
        rec.emit("v", lambda v: v.tensor_tensor(qnw_s[0:1, :], qnw_s[0:1, :], knw_s[0:1, :], ALU.mult), "V")
        vwqk = rec.val("V")
        rec.emit("v", lambda v: v.tensor_reduce(mrow[0][0:1, 0:1], qnw_s[0:1, :], AX.X, ALU.max), "V")
        rec.emit("v", lambda v: v.tensor_reduce(mrow[0][0:1, 1:2], qnw_s[0:1, :], AX.X, ALU.min), "V")
        rec.emit("v", lambda v: v.tensor_scalar(mrow[0][0:1, 1:2], mrow[0][0:1, 1:2], -1.0, 0.0, ALU.mult, ALU.add), "V")
        rec.emit("v", lambda v: v.tensor_tensor(mrow[0][0:1, 0:1], mrow[0][0:1, 0:1], mrow[0][0:1, 1:2], ALU.max), "V")
        rec.emit("v", lambda v: v.tensor_scalar(mrow[1][0:1, 0:1], mrow[0][0:1, 0:1], -float(HD), 0.0, ALU.mult, ALU.add), "V")
        rec.emit("v", lambda v: v.tensor_scalar(qib_s[0:1, :], qib_s[0:1, :], mrow[1][0:1, 0:1], 0.0, ALU.add, ALU.add), "V")
        qshift_v = rec.val("V")
        rec.wait("t", "V", vwqk)
        rec.wait("t", "dG", ld0)
        rec.emit("t", lambda t: t.matmul(p01[0][:, 0:HD], onesf_s[0:1, :], qnw_s[0:1, :], start=True, stop=True), "T")
        twqk = rec.val("T")
        rec.wait("s", "T", twqk)
        rec.emit("s", lambda s: s.copy(wqk_s[:, :], p01[0][:, 0:HD]), "S")
        swqk = rec.val("S")
        st[("evS", 0)] = swqk

        def dummy_out():
            rec.wait("v", "dG", rec.val("dG"))
            rec.emit("v", lambda v: v.memset(sq_scr[:, :], 0.0), "V")
            dv = rec.val("V")
            rec.wait("y", "V", dv)
            for t_ in range(NTO):
                for off in range(0, DM, SQW):
                    w_ = min(SQW, DM - off)
                    dma("y", lambda g, t_=t_, off=off, w_=w_: g.dma_start(
                        out=yout[t_ * 128:(t_ + 1) * 128, off:off + w_], in_=sq_scr[:, 0:w_]))
            rec.wait("y", "dS", rec.val("dS"))
            rec.wait("p", "dG", rec.val("dG"))
            block = e(nc.Block())
            block.gpsimd(lambda g: _replay(rec, "p", g, sems))
            block.sync(lambda y: _replay(rec, "y", y, sems))
            block.tensor(lambda t: _replay(rec, "t", t, sems))
            block.scalar(lambda s: _replay(rec, "s", s, sems))
            block.vector(lambda v: _replay(rec, "v", v, sems))

        if upto == 0:
            dummy_out()
            return nc

        def ln_stats(getbuf, pp=0):
            rec.wait("v", "S", st.get(("lnS", pp), 0))
            rec.wait("s", "V", st.get(("lnV", pp), 0))
            rec.emit("v", lambda v, pp=pp: v.tensor_reduce(ln_sum[pp][:, :], getbuf()[:, :], AX.X, ALU.add), "V")
            DM2 = DM // 2
            rec.emit("s", lambda s, pp=pp: s.activation(sq_scr[:, 0:DM2], getbuf()[:, 0:DM2], AF.Square, accum_out=ln_ssq[pp][:, :]), "S")
            rec.emit("s", lambda s, pp=pp: s.activation(sq_scr[:, 0:DM2], getbuf()[:, DM2:DM], AF.Square, accum_out=ln_h2[pp][:, :]), "S")
            ss = rec.val("S")
            rec.wait("v", "S", ss)
            rec.emit("v", lambda v, pp=pp: v.tensor_tensor(ln_ssq[pp][:, :], ln_ssq[pp][:, :], ln_h2[pp][:, :], ALU.add), "V")
            rec.emit("v", lambda v, pp=pp: v.tensor_scalar(ln_mean[pp][:, :], ln_sum[pp][:, :], 1.0 / DM, 0.0, ALU.mult, ALU.add), "V")
            rec.emit("v", lambda v, pp=pp: v.tensor_tensor(ln_t1[pp][:, :], ln_mean[pp][:, :], ln_mean[pp][:, :], ALU.mult), "V")
            rec.emit("v", lambda v, pp=pp: v.tensor_scalar(ln_var[pp][:, :], ln_ssq[pp][:, :], 1.0 / DM, 0.0, ALU.mult, ALU.add), "V")
            rec.emit("v", lambda v, pp=pp: v.tensor_tensor(ln_var[pp][:, :], ln_var[pp][:, :], ln_t1[pp][:, :], ALU.subtract), "V")
            vv = rec.val("V")
            st[("lnV", pp)] = vv
            rec.wait("s", "V", vv)
            rec.emit("s", lambda s, pp=pp: s.activation(ln_t1[pp][:, :], ln_var[pp][:, :], AF.Ln, bias=cv[:, 0:1]), "S")
            rec.emit("s", lambda s, pp=pp: s.activation(ln_rstd[pp][:, :], ln_t1[pp][:, :], AF.Exp, scale=cv[:, 2:3]), "S")
            st[("lnS", pp)] = rec.val("S")
            return rec.val("S")

        dma("y", lambda g: g.dma_start(out=lnxT[:, :], in_=xta), "dX0")
        xtv = rec.val("dX0")
        rec.emit("v", lambda v: v.memset(onesb[:, :], 1.0), "V")
        rec.emit("v", lambda v: v.memset(onescol[:, :], 1.0), "V")
        vmem = rec.val("V")
        rec.wait("t", "dX0", xtv)
        rec.wait("t", "V", vmem)
        for t in range(NTO):
            rec.wait("t", "V", st.get("p0sumV", 0))
            for k in range(DMC):
                rec.emit("t", lambda tt, k=k, t=t: tt.matmul(
                    PB[6][:, 0:1], lnxT[:, k * TOK + t * 128:k * TOK + (t + 1) * 128],
                    onescol[:, :], start=(k == 0), stop=(k == DMC - 1)), "T")
            tsv = rec.val("T")
            rec.wait("v", "T", tsv)
            rec.emit("v", lambda v: v.tensor_scalar(negmub[:, :], PB[6][:, 0:1], -1.0 / DM, 0.0, ALU.mult, ALU.add), "V")
            st["p0sumV"] = rec.val("V")
            rec.wait("t", "V", st["p0sumV"])
            rec.wait("t", "V", st.get("p0rowV", 0))
            rec.emit("t", lambda tt: tt.matmul(PB[7][0:1, 0:128], negmub[:, :], identb_s[:, :], start=True, stop=True), "T")
            trv = rec.val("T")
            rec.wait("v", "T", trv)
            rec.emit("v", lambda v, t=t: v.tensor_copy(nmrow[0:1, t * 128:(t + 1) * 128], PB[7][0:1, 0:128]), "V")
            st["p0rowV"] = rec.val("V")
        rec.wait("t", "V", st["p0rowV"])
        CG = min(512, TOK)
        NH = TOK // CG
        for k in range(DMC):
            for h in range(NH):
                pb_ = pST_pre[(k * NH + h) % 2]
                rec.wait("t", "V", st.get(("p0c", (k * NH + h) % 2), 0))
                rec.emit("t", lambda tt, pb_=pb_, h=h, CG=CG: tt.matmul(
                    pb_[:, 0:CG], onesb[:, :], nmrow[0:1, h * CG:(h + 1) * CG], start=True, stop=True), "T")
                tcv = rec.val("T")
                rec.wait("v", "T", tcv)
                rec.emit("v", lambda v, pb_=pb_, k=k, h=h, CG=CG: v.tensor_tensor(
                    lnxT[:, k * TOK + h * CG:k * TOK + (h + 1) * CG],
                    lnxT[:, k * TOK + h * CG:k * TOK + (h + 1) * CG], pb_[:, 0:CG], ALU.add), "V")
                st[("p0c", (k * NH + h) % 2)] = rec.val("V")
        rec.wait("t", "V", rec.val("V"))
        for t in range(NTO):
            rec.wait("t", "V", st.get("p0gV", 0))
            for k in range(DMC):
                rec.emit("t", lambda tt, k=k, t=t: tt.matmul(
                    PB[6][:, 128:256], lnxT[:, k * TOK + t * 128:k * TOK + (t + 1) * 128],
                    lnxT[:, k * TOK + t * 128:k * TOK + (t + 1) * 128],
                    start=(k == 0), stop=(k == DMC - 1)), "T")
            tgv = rec.val("T")
            rec.wait("v", "T", tgv)
            rec.emit("v", lambda v: v.tensor_tensor(sq_scr[:, 0:128], PB[6][:, 128:256], identf_s[:, :], ALU.mult), "V")
            rec.emit("v", lambda v, t=t: v.tensor_reduce(rstd5[:, t:t + 1], sq_scr[:, 0:128], AX.X, ALU.add), "V")
            st["p0gV"] = rec.val("V")
            rec.wait("s", "V", st["p0gV"])
            rec.emit("s", lambda s, t=t: s.activation(rstd5[:, t:t + 1], rstd5[:, t:t + 1], AF.Ln, bias=cv[:, 0:1], scale=cv[:, 5:6]), "S")
            rec.emit("s", lambda s, t=t: s.activation(rstd5[:, t:t + 1], rstd5[:, t:t + 1], AF.Exp, scale=cv[:, 2:3]), "S")
        p0S = rec.val("S")
        p0V = rec.val("V")
        rec.wait("v", "S", p0S)
        rec.wait("s", "V", p0V)
        rec.wait("t", "V", p0V)
        p0rd = [rec.val("dT0"), rec.val("dT1")]
        for i_ in range(6):
            st[("evV", i_)] = p0V
        wch = [carve(98, [128, DMC * 512], BF16), carve(130, [128, DMC * 512], BF16)]

        if upto == 1:
            dummy_out()
            return nc
        wsem_i = {"i": 0}

        def proj_prefetch(wt_ap, cc):
            wb = cc % 2
            wsem = "dW%d" % (wsem_i["i"] % 4)
            wsem_i["i"] += 1
            rec.wait("p", "T", st.get(("wchfree", wb), 0))
            dma("p", lambda g, wb=wb, cc=cc, wt_ap=wt_ap: g.dma_start(
                out=wch[wb][:, 0:DMC * 512],
                in_=wt_ap[:, cc * DMC * 512:(cc + 1) * DMC * 512]), wsem)
            return (wsem, rec.val(wsem), wb)

        def proj_phase(wt_ap, ncols, out_cb, name, chunk_done=None, pf0=None, chunk_pre=None):
            ncc = ncols // 512
            for cc in range(ncc):
                if cc == 0 and pf0 is not None:
                    wsem, dwv, wb = pf0
                else:
                    wsem, dwv, wb = proj_prefetch(wt_ap, cc)
                if chunk_pre is not None and cc > 0:
                    chunk_pre(cc - 1)
                for wave in range(0, NTO, WAVE):
                    tiles = list(range(wave, min(wave + WAVE, NTO)))
                    for k in range(DMC):
                        for t in tiles:
                            pb = p01[t % 6]
                            rec.wait("t", wsem, dwv)
                            rec.wait("t", "dT0", p0rd[0])
                            rec.wait("t", "dT1", p0rd[1])
                            rec.wait("t", "V", st.get(("evV", t % 6), 0))
                            rec.wait("t", "S", st.get(("evS", t % 6), 0))
                            rec.emit("t", lambda tt, pb=pb, k=k, t=t, wb=wb: tt.matmul(
                                pb[:, :], lnxT[:, k * TOK + t * 128: k * TOK + (t + 1) * 128],
                                wch[wb][:, k * 512:(k + 1) * 512],
                                start=(k == 0), stop=(k == DMC - 1)), "T")
                    st[("wchfree", wb)] = rec.val("T")
                    tmm = rec.val("T")
                    for t in tiles:
                        out_cb(t, cc, p01[t % 6], tmm)
                        st[("evV", t % 6)] = rec.val("V")
                        st[("evS", t % 6)] = rec.val("S")
                if chunk_done is not None:
                    chunk_done(cc)
            if chunk_pre is not None:
                chunk_pre(ncc - 1)

        def rms_evac(pb, tmm, pr, t):
            rec.wait("s", "T", tmm)
            rec.wait("s", "V", st.get(("rmsV", pr), 0))
            rec.emit("s", lambda s, pb=pb, pr=pr: s.activation(sq_scr[:, pr * 512:pr * 512 + 512], pb[:, :], AF.Square), "S")
            sv_ = rec.val("S")
            rec.wait("v", "S", sv_)
            rec.emit("v", lambda v, pr=pr: v.tensor_reduce(rms_ssq[pr][:, :], sq_scr[:, pr * 512:pr * 512 + 512].rearrange("p (h d) -> p h d", d=128), AX.X, ALU.add), "V")
            rec.emit("v", lambda v, pr=pr, t=t: v.tensor_scalar(rms_ssq[pr][:, :], rms_ssq[pr][:, :], rstd5[:, t:t + 1], rstd5[:, t:t + 1], ALU.mult, ALU.mult), "V")
            vr = rec.val("V")
            rec.wait("s", "V", vr)
            rec.emit("s", lambda s, pr=pr: s.activation(rms_ssq[pr][:, :], rms_ssq[pr][:, :], AF.Ln, bias=cv[:, 1:2], scale=cv[:, 3:4]), "S")
            rec.emit("s", lambda s, pr=pr: s.activation(rms_rstd[pr][:, :], rms_ssq[pr][:, :], AF.Exp, scale=cv[:, 2:3]), "S")
            sr = rec.val("S")
            rec.wait("v", "S", sr)
            rec.emit("v", lambda v, pr=pr, t=t: v.tensor_scalar(rms_rstd[pr][:, :], rms_rstd[pr][:, :], rstd5[:, t:t + 1], 0.0, ALU.mult, ALU.add), "V")

        KQSEM = ["dX0", "dX1", "dY0", "dY1"]

        def k_evac(t, cc, pb, tmm):
            par = t % 4
            px = t % 2
            rms_evac(pb, tmm, par, t)
            rec.wait("v", KQSEM[par], st.get(("ktmp_st", par), 0))
            for h in range(4):
                rec.emit("v", lambda v, pb=pb, h=h, par=par: v.scalar_tensor_tensor(
                    ktmp[par][:, h * 128:(h + 1) * 128],
                    pb[:, h * 128:(h + 1) * 128], rms_rstd[par][:, h:h + 1], wqk_s[:, :],
                    ALU.mult, ALU.mult), "V")
            kvv = rec.val("V")
            st[("rmsV", par)] = kvv
            rec.wait("y", "V", kvv)
            dma("y", lambda g, t=t, cc=cc, par=par: g.dma_start(
                out=ktok[t * 128:(t + 1) * 128, cc * 512:(cc + 1) * 512], in_=ktmp[par][:, :]), KQSEM[par])
            st[("ktmp_st", par)] = rec.val(KQSEM[par])

        kpub_box = {}

        def v_evac(t, cc, pb, tmm):
            rec.wait("v", "T", tmm)
            for kc in range(KVD // 512):
                rec.wait("v", kpub_box[kc][0], kpub_box[kc][1])
            rec.emit("v", lambda v, pb=pb, t=t, cc=cc: v.tensor_scalar(
                kvsl[:, t * KVD + cc * 512: t * KVD + (cc + 1) * 512], pb[:, :],
                rstd5[:, t:t + 1], 0.0, ALU.mult, ALU.add), "V")

        def q_evac(t, cc, pb, tmm):
            par = t % 4
            px = t % 2
            rms_evac(pb, tmm, par, t)
            rec.wait("v", KQSEM[par], st.get(("ktmp_st", par), 0))
            for h in range(4):
                rec.emit("v", lambda v, pb=pb, h=h, par=par: v.tensor_scalar(
                    ktmp[par][:, h * 128:(h + 1) * 128],
                    pb[:, h * 128:(h + 1) * 128], rms_rstd[par][:, h:h + 1], 0.0, ALU.mult, ALU.add), "V")
            kvv = rec.val("V")
            st[("rmsV", par)] = kvv
            rec.wait("y", "V", kvv)
            rec.wait("y", "dT1", st.get(("qtok_rd", cc % 2), 0))
            dma("y", lambda g, t=t, cc=cc, par=par: g.dma_start(
                out=qtok[t * 128:(t + 1) * 128, (cc % 2) * 512:(cc % 2) * 512 + 512], in_=ktmp[par][:, :]), KQSEM[par])
            st[("ktmp_st", par)] = rec.val(KQSEM[par])

        def k_pub_chunk(cc):
            for sm in KQSEM:
                rec.wait("y", sm, rec.val(sm))
            for kv in range(4 * cc, 4 * cc + 4):
                for t in range(NTO):
                    dma("y", lambda g, kv=kv, t=t: g.dma_start(
                        out=kvsl[:, kv * TOK + t * 128: kv * TOK + (t + 1) * 128],
                        in_=ktok[t * 128:(t + 1) * 128, kv * 128:(kv + 1) * 128], transpose=True), "dT0")
            rec.wait("y", "dT0", rec.val("dT0"))
            dma("y", lambda g, cc=cc: g.dma_start(
                out=kag_i[cc * 512:(cc + 1) * 512, :].rearrange("(kv p) t -> p kv t", p=128),
                in_=kvsl[:, 4 * cc * TOK:(4 * cc + 4) * TOK].rearrange("p (kv t) -> p kv t", t=TOK)),
                "dP%d" % (cc % 2))
            kpub_box[cc] = ("dP%d" % (cc % 2), rec.val("dP%d" % (cc % 2)))

        def k_chunk_ag(cc):
            rec.wait("p", kpub_box[cc][0], kpub_box[cc][1])
            rec.emit("p", lambda g, cc=cc: g.collective_compute(
                "AllGather", ALU.bypass, replica_groups=[list(range(NC))],
                ins=[kag_i[cc * 512:(cc + 1) * 512, :].opt()],
                outs=[kag_o[cc * NC * 512:(cc + 1) * NC * 512, :].opt()]), "CC")

        proj_phase(wkt, KVD, k_evac, "K", chunk_done=k_pub_chunk, chunk_pre=k_chunk_ag)
        pf_q = proj_prefetch(wqt, 0)
        cc_k = rec.val("CC")

        def q_publish(cc):
            half = (cc % 2) * NTO * 512
            for sm in KQSEM:
                rec.wait("y", sm, rec.val(sm))
            for h in range(4):
                for t in range(NTO):
                    dma("y", lambda g, h=h, t=t, cc=cc, half=half: g.dma_start(
                        out=qnsl[:, half + h * TOK + t * 128: half + h * TOK + (t + 1) * 128],
                        in_=qtok[t * 128:(t + 1) * 128, (cc % 2) * 512 + h * 128:(cc % 2) * 512 + (h + 1) * 128],
                        transpose=True), "dT1")
            rdv = rec.val("dT1")
            st[("qtok_rd", cc % 2)] = rdv
            rec.wait("y", "dT1", rdv)
            qsem = "dP%d" % (cc % 2)
            dma("y", lambda g, cc=cc, half=half: g.dma_start(
                out=qag_i[cc * 512:(cc + 1) * 512, :].rearrange("(hl p) t -> p hl t", p=128),
                in_=qnsl[:, half:half + NTO * 512].rearrange("p (hl t) -> p hl t", t=TOK)), qsem)
            st[("qpub", cc % 2)] = (qsem, rec.val(qsem))

        def q_chunk_ag(cc):
            qsem, qv = st[("qpub", cc % 2)]
            rec.wait("p", qsem, qv)
            rec.emit("p", lambda g, cc=cc: g.collective_compute(
                "AllGather", ALU.bypass, replica_groups=[list(range(NC))],
                ins=[qag_i[cc * 512:(cc + 1) * 512, :].opt()],
                outs=[qag_o[cc * NC * 512:(cc + 1) * NC * 512, :].opt()]), "CC")

        proj_phase(wqt, HDTOT, q_evac, "Q", chunk_done=q_publish, pf0=pf_q, chunk_pre=q_chunk_ag)
        pf_v = proj_prefetch(wvt, 0)
        cc_q = rec.val("CC")

        proj_phase(wvt, KVD, v_evac, "V", pf0=pf_v)
        vS = rec.val("V")
        rec.wait("y", "V", vS)
        dma("y", lambda g: g.dma_start(
            out=vag_i.rearrange("(t p) k -> p t k", p=128),
            in_=kvsl[:, :].rearrange("p (t k) -> p t k", k=KVD)))
        vpub = rec.val("dS")
        rec.wait("p", "dS", vpub)
        rec.emit("p", lambda g: g.collective_compute(
            "AllGather", ALU.bypass, replica_groups=[list(range(NC))],
            ins=[vag_i.ap().opt()], outs=[vag_o.ap().opt()]), "CC")
        cc_v = rec.val("CC")

        if upto == 2:
            rec.wait("p", "CC", rec.val("CC"))
            dummy_out()
            return nc
        phase_barrier()

        KSZ = S * 2 // 1024
        ksb = [carve(0, [128, S], BF16), carve(KSZ, [128, S], BF16)]
        vsb = [carve(2 * KSZ, [128, NSUB * 129], BF16),
               carve(2 * KSZ + (NSUB * 129 * 2 + 1023) // 1024 + 1, [128, NSUB * 129], BF16)]
        _v2 = 2 * KSZ + 2 * ((NSUB * 129 * 2 + 1023) // 1024 + 1)
        qsb = [carve(_v2, [128, S], BF16), carve(_v2 + KSZ, [128, S], BF16)]
        otg = carve(_v2 + 2 * KSZ, [128, S], BF16)
        pST = PB[0:4]
        pO_full = PB[4:6]
        pO_ = [t[:, 0:129] for t in pO_full]
        pTRb = PBb

        rec.emit("v", lambda v: v.memset(vsb[0][:, :], 1.0), "V")
        rec.emit("v", lambda v: v.memset(vsb[1][:, :], 1.0), "V")
        vones = rec.val("V")

        def gather_g(g_idx, bp):
            asem = "dA%d" % bp
            qsem = "dP%d" % bp
            tfree = rec.val("T")
            rec.wait("p", "T", tfree)
            rec.wait("p", "CC", cc_k)
            dma("p", lambda g, g_idx=g_idx, bp=bp: g.dma_start(
                out=ksb[bp][:, :].rearrange("p (r t) -> p r t", t=TOK),
                in_=kag_o.ap()[ds(dyn["k%d" % g_idx], NC * 512), :].rearrange(
                    "(j w) t -> w j t", w=512)[0:128, :, :]), asem)
            rec.wait("y", "T", tfree)
            rec.wait("y", "CC", cc_q)
            dma("y", lambda g, g_idx=g_idx, bp=bp: g.dma_start(
                out=qsb[bp][:, :].rearrange("p (r t) -> p r t", t=TOK),
                in_=qag_o.ap()[ds(dyn["q%d_sp" % g_idx], NC * 512), :].rearrange(
                    "(j w) t -> w j t", w=512)[0:128, :, :]), qsem)
            v_k = rec.val(asem)
            v_q = rec.val(qsem)
            vsem = "dW%d" % (2 + bp)
            rec.wait("p", "V", vones)
            rec.wait("p", "CC", cc_v)
            dma("p", lambda g, g_idx=g_idx, bp=bp: g.dma_start(
                out=vsb[bp][:, :].rearrange("p (jj c) -> p jj c", c=129)[:, :, 0:128],
                in_=vag_o[:, ds(dyn["v%d" % g_idx], 128)].rearrange("(jj p) d -> p jj d", p=128)), vsem)
            return (asem, v_k, vsem, rec.val(vsem), qsem, v_q)

        rec.wait("t", "dG", ld0)
        job_i = {"i": 0}
        NPST = 4

        def emit_preST(g_idx, T, par, bp, stg_v, slot, ci, gath_v):
            pst = pST[slot]
            ji = job_i["i"]
            job_i["i"] += 1
            rec.wait("t", gath_v[0], gath_v[1])
            rec.wait("t", gath_v[4], gath_v[5])
            rec.wait("t", "V", stg_v)
            rec.wait("t", "S", st.get(("stS", slot), 0))
            rec.wait("t", "V", st.get(("pst_busyV", slot), 0))
            rec.emit("t", lambda tt, pst=pst, g_idx=g_idx, par=par: tt.matmul(
                pst[:, :], lhs3_s[:, g_idx * 128:(g_idx + 1) * 128], stag[par][:, :],
                start=True, stop=False), "T")
            for j in range(4):
                rec.emit("t", lambda tt, pst=pst, ci=ci, j=j, T=T, bp=bp: tt.matmul(
                    pst[:, j * 128:(j + 1) * 128],
                    ksb[bp][:, ci * 512 + j * 128: ci * 512 + (j + 1) * 128],
                    qsb[bp][:, T * 128:(T + 1) * 128], start=False, stop=(j == 3)), "T")
            tmm = rec.val("T")
            c_lo, c_hi = chunk_range(cfg, g_idx, T)
            if ci == c_hi:
                phase = T % 4
                rec.wait("v", "T", tmm)
                rec.emit("v", lambda v, pst=pst, phase=phase: v.copy_predicated(
                    pst[:, :], masks_s[:, phase * 512:(phase + 1) * 512], neg_f[:, :]), "V")
                vmask = rec.val("V")
                rec.wait("s", "V", vmask)
                st[("pst_busyV", slot)] = vmask
            ab = abuf[slot]
            rec.wait("s", "T", tmm)
            rec.wait("s", "T", st.get(("ab_busyT", slot), 0))
            rec.emit("s", lambda s, ab=ab, pst=pst, ji=ji: s.activation(
                ab[:, :], pst[:, :], AF.Exp, bias=btab_s[:, ji:ji + 1], scale=cv[:, 4:5]), "S")
            st[("stS", slot)] = rec.val("S")
            return rec.val("S")

        def evac_T(T, par):
            po = pO_[par]
            tdone = rec.val("T")
            rec.wait("v", "T", tdone)
            ob = st.get("outh_busy_dX")
            if ob is not None:
                rec.wait("v", "dX%d" % ob[0], ob[1])
            rec.wait("v", "dS", st.get("otg_busy", 0))
            rec.emit("v", lambda v, po=po: v.reciprocal(rcp_s[:, :], po[:, 128:129]), "V")
            rec.emit("v", lambda v, po=po: v.tensor_scalar(outh_b[:, :], po[:, 0:128], rcp_s[:, :], 0.0, ALU.mult, ALU.add), "V")
            vev = rec.val("V")
            st[("ofree", par)] = vev
            slot = T % 2
            rec.wait("y", "V", vev)
            rec.wait("y", "dT%d" % slot, st.get(("oscr_rd", slot), 0))
            dma("y", lambda g, slot=slot: g.dma_start(
                out=oscr[slot * 128:(slot + 1) * 128, :], in_=outh_b[:, :]), "dX%d" % slot)
            stv = rec.val("dX%d" % slot)
            st["outh_busy_dX"] = (slot, stv)
            rec.wait("y", "dX%d" % slot, stv)
            dma("y", lambda g, slot=slot, T=T: g.dma_start(
                out=otg[:, T * 128:(T + 1) * 128],
                in_=oscr[slot * 128:(slot + 1) * 128, :], transpose=True), "dT%d" % slot)
            st[("oscr_rd", slot)] = rec.val("dT%d" % slot)

        def emit_pv(jb, bp, gvv):
            (T, par, ci, first, last, slot, expv) = jb
            po = pO_[par]
            rec.wait("t", gvv[2], gvv[3])
            if first:
                rec.wait("t", "V", st.get(("ofree", par), 0))
            rec.wait("t", "S", expv)
            ab = abuf[slot]
            for j in range(4):
                rec.emit("t", lambda tt, po=po, ab=ab, ci=ci, j=j, bp=bp, first=first, last=last: tt.matmul(
                    po[:, :], ab[:, j * 128:(j + 1) * 128],
                    vsb[bp][:, (4 * ci + j) * 129:(4 * ci + j) * 129 + 129],
                    start=(first and j == 0), stop=(last and j == 3)), "T")
            st[("ab_busyT", slot)] = rec.val("T")
            if last:
                evac_T(T, par)

        gv = {0: gather_g(0, 0)}
        for g_idx in range(GS):
            bp = g_idx % 2
            if g_idx + 1 < GS:
                gv[g_idx + 1] = gather_g(g_idx + 1, 1 - bp)
            rec.wait("v", "T", st.get("stag_busyT", 0))
            rec.wait("v", "V", qshift_v)
            for par in range(2):
                rec.emit("v", lambda v, par=par, g_idx=g_idx: v.tensor_copy(
                    stag[par][0:1, :], qib_s[0:1, g_idx * 512:(g_idx + 1) * 512]), "V")
            stg_v = rec.val("V")
            jn = 0
            prev = None
            for T in range(NT):
                par = T % 2
                c_lo, c_hi = chunk_range(cfg, g_idx, T)
                for ci in range(c_lo, c_hi + 1):
                    slot = jn % NPST
                    expv = emit_preST(g_idx, T, par, bp, stg_v, slot, ci, gv[g_idx])
                    jb = (T, par, ci, ci == c_lo, ci == c_hi, slot, expv)
                    if prev is not None:
                        emit_pv(prev, bp, gv[g_idx])
                    prev = jb
                    jn += 1
            emit_pv(prev, bp, gv[g_idx])
            st["stag_busyT"] = rec.val("T")
            st["ksb_free_%d" % bp] = rec.val("T")
            rec.wait("y", "dT0", rec.val("dT0"))
            rec.wait("y", "dT1", rec.val("dT1"))
            dma("y", lambda g, g_idx=g_idx: g.dma_start(
                out=oag_i[g_idx * 128:(g_idx + 1) * 128, :], in_=otg[:, :]))
            st["otg_busy"] = rec.val("dS")
            rec.wait("p", "dS", st["otg_busy"])
            rec.emit("p", lambda g, g_idx=g_idx: g.collective_compute(
                "AllGather", ALU.bypass, replica_groups=[list(range(NC))],
                ins=[oag_i[g_idx * 128:(g_idx + 1) * 128, :].opt()],
                outs=[oag_o[g_idx * NC * 128:(g_idx + 1) * NC * 128, :].opt()]), "CC")
            st[("cc_o", g_idx)] = rec.val("CC")
        cc_o = rec.val("CC")

        if upto == 3:
            rec.wait("p", "CC", rec.val("CC"))
            dummy_out()
            return nc
        phase_barrier()

        otall = carve(0, [128, NTO * H * 128], BF16)
        wrow = [carve(64, [128, H * 512], BF16), carve(64 + H, [128, H * 512], BF16)]
        attst = [carve(64 + 2 * H, [128, 512], F32), carve(66 + 2 * H, [128, 512], F32)]
        p4 = PB[0:min(NTO, 8)]

        OGW = NTO * NC * 128
        for gg in range(GS):
            rec.wait("p", "CC", st[("cc_o", gg)])
            dma("p", lambda g, gg=gg: g.dma_start(
                out=oag_l[gg * NC * 128:(gg + 1) * NC * 128, :],
                in_=oag_o[gg * NC * 128:(gg + 1) * NC * 128, :][:, ds(dyn["oo"], TOK)]))
        olv = rec.val("dG")
        rec.wait("p", "dG", olv)
        for gg in range(GS):
            for jj in range(NC):
                dma("p", lambda g, gg=gg, jj=jj: g.dma_start(
                    out=otall[:, gg * OGW + jj * NTO * 128: gg * OGW + (jj + 1) * NTO * 128],
                    in_=oag_l[gg * NC * 128 + jj * 128: gg * NC * 128 + (jj + 1) * 128, :]))
        otv = rec.val("dG")

        for dc in range(DC):
            wb = dc % 2
            wsem = "dW%d" % wb
            rec.wait("p", "T", st.get(("wrow_busy", wb), 0))
            dma("p", lambda g, dc=dc, wb=wb: g.dma_start(
                out=wrow[wb][:, :],
                in_=wot[:, dc * H * 512:(dc + 1) * H * 512]), wsem)
            dwv = rec.val(wsem)
            for rg in range(H):
                for t in range(NTO):
                    rec.wait("t", wsem, dwv)
                    rec.wait("t", "dG", otv)
                    if rg == 0:
                        rec.wait("t", "S" if t % 2 == 0 else "V", st.get(("p4free", t % 8), 0))
                    oco = (rg % GS) * OGW + (rg // GS) * NTO * 128 + t * 128
                    rec.emit("t", lambda tt, t=t, rg=rg, wb=wb, oco=oco: tt.matmul(
                        p4[t % 8][:, :], otall[:, oco: oco + 128],
                        wrow[wb][:, rg * 512:(rg + 1) * 512], start=(rg == 0), stop=(rg == H - 1)), "T")
            st[("wrow_busy", wb)] = rec.val("T")
            twv = rec.val("T")
            for t in range(NTO):
                ap_ = t % 2
                if ap_ == 0:
                    rec.wait("s", "T", twv)
                    rec.wait("s", "dX0", st.get(("attst_busy", 0), 0))
                    rec.emit("s", lambda s, t=t: s.copy(attst[0][:, :], p4[t % 8][:, :]), "S")
                    sv_ = rec.val("S")
                    rec.wait("y", "S", sv_)
                else:
                    rec.wait("v", "T", twv)
                    rec.wait("v", "dX1", st.get(("attst_busy", 1), 0))
                    rec.emit("v", lambda v, t=t: v.tensor_copy(attst[1][:, :], p4[t % 8][:, :]), "V")
                    sv_ = rec.val("V")
                    rec.wait("y", "V", sv_)
                st[("p4free", t % 8)] = sv_
                dma("y", lambda g, t=t, dc=dc, ap_=ap_: g.dma_start(
                    out=attd[t * 128:(t + 1) * 128, dc * 512:(dc + 1) * 512], in_=attst[ap_][:, :]), "dX%d" % ap_)
                st[("attst_busy", ap_)] = rec.val("dX%d" % ap_)
        if upto == 4:
            dummy_out()
            return nc
        p4dma = rec.val("dS")
        phase_barrier()

        x5 = [carve(0, [128, DM], F32), carve(16, [128, DM], F32)]
        at5 = [carve(32, [128, DM], F32), carve(48, [128, DM], F32)]
        x2b = [carve(64, [128, DM], F32), carve(80, [128, DM], F32)]
        ffb = carve(96, [128, DM], F32)
        kfb = carve(112, [128, RC * DM], BF16)
        _o4 = 112 + RC * DM * 2 // 1024
        s8b = carve(_o4, [128, RDIM], F32)
        _o5 = _o4 + (RDIM * 4 + 1023) // 1024 + 1
        s8c = carve(_o5, [128, 2 * RDIM], F32)
        _o6 = _o5 + (2 * RDIM * 4 + 1023) // 1024 + 1
        zbf = [carve(_o6, [128, RDIM], BF16),
               carve(_o6 + (RDIM * 2 + 1023) // 1024 + 1, [128, RDIM], BF16)]
        _o7 = _o6 + 2 * ((RDIM * 2 + 1023) // 1024 + 1)
        zT = [carve(_o7, [128, RC * 128], BF16),
              carve(_o7 + (RC * 128 * 2 + 1023) // 1024 + 1, [128, RC * 128], BF16)]
        outb = x5
        p5 = PB[0:2]

        dma("p", lambda g: g.dma_start(out=kfb[:, :], in_=kfh))
        kfv = rec.val("dG")

        def p5_stageA(t):
            b = t % 2
            rec.wait("y", "dS", p4dma)
            rec.wait("y", "V", st.get(("p5free", b), 0))
            rec.wait("y", "S", st.get(("p5freeS", b), 0))
            rec.wait("y", "dX%d" % b, st.get(("outb_busy", b), 0))
            xsem = "dX%d" % b
            dma("y", lambda g, t=t, b=b: g.dma_start(out=x5[b][:, :], in_=xa[t * 128:(t + 1) * 128, :]), xsem)
            dma("y", lambda g, t=t, b=b: g.dma_start(out=at5[b][:, :], in_=attd[t * 128:(t + 1) * 128, :]), xsem)
            dv = rec.val(xsem)
            rec.wait("v", xsem, dv)
            rec.wait("s", xsem, dv)
            srstd = ln_stats(lambda b=b: at5[b], b)
            rec.wait("v", "S", srstd)
            rec.emit("v", lambda v, b=b: v.tensor_scalar(at5[b][:, :], at5[b][:, :], ln_mean[b][:, :], ln_rstd[b][:, :], ALU.subtract, ALU.mult), "V")
            rec.emit("v", lambda v, b=b: v.tensor_tensor(x2b[b][:, :], x5[b][:, :], at5[b][:, :], ALU.add), "V")
            vx2 = rec.val("V")
            st[("lnV", b)] = vx2
            rec.wait("s", "V", vx2)
            srstd3 = ln_stats(lambda b=b: x2b[b], b)
            rec.wait("v", "S", srstd3)
            rec.emit("v", lambda v, b=b: v.tensor_scalar(ln_t1[b][:, :], ln_mean[b][:, :], 8.0, 0.0, ALU.mult, ALU.add), "V")
            R_ = RDIM
            rec.wait("p", "V", vx2)
            rec.wait("p", "V", st.get("s8bV", 0))
            rec.emit("p", lambda g, b=b: g.tensor_tensor(s8c[:, 0:R_], x2b[b][:, 0:R_], x2b[b][:, R_:2 * R_], ALU.add), "P")
            rec.emit("p", lambda g, b=b: g.tensor_tensor(s8c[:, R_:2 * R_], x2b[b][:, 2 * R_:3 * R_], x2b[b][:, 3 * R_:4 * R_], ALU.add), "P")
            rec.emit("p", lambda g, b=b: g.tensor_tensor(s8b[:, :], x2b[b][:, 4 * R_:5 * R_], x2b[b][:, 5 * R_:6 * R_], ALU.add), "P")
            rec.emit("p", lambda g, b=b: g.tensor_tensor(s8b[:, :], s8b[:, :], x2b[b][:, 6 * R_:7 * R_], ALU.add), "P")
            rec.emit("p", lambda g, b=b: g.tensor_tensor(s8b[:, :], s8b[:, :], x2b[b][:, 7 * R_:8 * R_], ALU.add), "P")
            rec.emit("p", lambda g: g.tensor_tensor(s8c[:, 0:R_], s8c[:, 0:R_], s8c[:, R_:2 * R_], ALU.add), "P")
            rec.emit("p", lambda g: g.tensor_tensor(s8b[:, :], s8b[:, :], s8c[:, 0:R_], ALU.add), "P")
            pv8 = rec.val("P")
            rec.wait("v", "P", pv8)
            rec.wait("v", "dX0", rec.val("dX0"))
            rec.wait("v", "dX1", rec.val("dX1"))
            rec.emit("v", lambda v, b=b: v.tensor_scalar(zbf[b][:, :], s8b[:, :], ln_t1[b][:, :], ln_rstd[b][:, :], ALU.subtract, ALU.mult), "V")
            vz = rec.val("V")
            st[("lnV", b)] = vz
            st["s8bV"] = vz
            rec.wait("y", "V", vz)
            rec.wait("y", "T", st.get(("zTfree", b), 0))
            for r in range(RC):
                slot = r % 2
                rec.wait("y", "dT%d" % slot, st.get(("z_rd", slot), 0))
                dma("y", lambda g, r=r, slot=slot, b=b: g.dma_start(
                    out=oscr[slot * 128:(slot + 1) * 128, :], in_=zbf[b][:, r * 128:(r + 1) * 128]), "dX%d" % slot)
                rec.wait("y", "dX%d" % slot, rec.val("dX%d" % slot))
                dma("y", lambda g, r=r, slot=slot, b=b: g.dma_start(
                    out=zT[b][:, r * 128:(r + 1) * 128],
                    in_=oscr[slot * 128:(slot + 1) * 128, :], transpose=True), "dT%d" % slot)
                st[("z_rd", slot)] = rec.val("dT%d" % slot)
            st[("vzT", b)] = [rec.val("dT0"), rec.val("dT1")]

        def p5_stageB(t):
            b = t % 2
            vzT_d = st[("vzT", b)]
            rec.wait("t", "dG", kfv)
            for dc in range(DC):
                pp = p5[dc % 2]
                rec.wait("t", "dT0", vzT_d[0])
                rec.wait("t", "dT1", vzT_d[1])
                rec.wait("t", "S", st.get(("ffS", dc % 2), 0))
                for r in range(RC):
                    rec.emit("t", lambda tt, pp=pp, r=r, dc=dc, b=b: tt.matmul(
                        pp[:, :], zT[b][:, r * 128:(r + 1) * 128],
                        kfb[:, r * DM + dc * 512: r * DM + (dc + 1) * 512],
                        start=(r == 0), stop=(r == RC - 1)), "T")
                tmm = rec.val("T")
                rec.wait("s", "T", tmm)
                rec.wait("s", "V", st.get("ffb_busyV", 0))
                rec.emit("s", lambda s, pp=pp, dc=dc: s.copy(ffb[:, dc * 512:(dc + 1) * 512], pp[:, :]), "S")
                st[("ffS", dc % 2)] = rec.val("S")
            st[("zTfree", b)] = rec.val("T")
            rec.wait("v", "S", st[("ffS", 0)])
            rec.wait("v", "S", st[("ffS", 1)])
            srstd4 = ln_stats(lambda: ffb, b)
            rec.wait("v", "S", srstd4)
            rec.emit("v", lambda v, b=b: v.tensor_scalar(ffb[:, :], ffb[:, :], ln_mean[b][:, :], ln_rstd[b][:, :], ALU.subtract, ALU.mult), "V")
            st[("lnV", b)] = rec.val("V")
            st[("p5freeS", b)] = rec.val("S")
            rec.wait("v", "dX%d" % b, st.get(("outb_busy", b), 0))
            rec.emit("v", lambda v, b=b: v.tensor_tensor(outb[b][:, :], ffb[:, :], x2b[b][:, :], ALU.add), "V")
            vout = rec.val("V")
            st["ffb_busyV"] = vout
            st[("p5free", b)] = vout
            rec.wait("y", "V", vout)
            dma("y", lambda g, t=t, b=b: g.dma_start(out=yout[t * 128:(t + 1) * 128, :], in_=outb[b][:, :]), "dX%d" % b)
            st[("outb_busy", b)] = rec.val("dX%d" % b)

        for t in range(NTO):
            p5_stageA(t)
            if t >= 1:
                p5_stageB(t - 1)
        p5_stageB(NTO - 1)

        for sem in ALLSEMS:
            rec.wait("y", sem, rec.val(sem))
        rec.wait("p", "dG", rec.val("dG"))
        rec.wait("p", "CC", rec.val("CC"))

        block = e(nc.Block())
        block.gpsimd(lambda g: _replay(rec, "p", g, sems))
        block.sync(lambda y: _replay(rec, "y", y, sems))
        block.tensor(lambda t: _replay(rec, "t", t, sems))
        block.scalar(lambda s: _replay(rec, "s", s, sems))
        block.vector(lambda v: _replay(rec, "v", v, sems))
    return nc


def _bf16(a):
    return np.asarray(a, np.float32).astype(ml_dtypes.bfloat16)


def build_core_inputs(cfg, c, x_full, Wq, Wk, Wv, Wo, qnw, knw, ffk, shared):
    TOK = cfg.TOK
    d = dict(shared)
    d["x"] = np.ascontiguousarray(x_full[c * TOK:(c + 1) * TOK, :], dtype=np.float32)
    DMC_ = cfg.DM // 128
    d["xt"] = _bf16(np.ascontiguousarray(
        d["x"].T.reshape(DMC_, 128, TOK).transpose(1, 0, 2).reshape(128, DMC_ * TOK)))

    heads = [8 * g + c for g in range(GS)]
    slopes = [slope_of(h) for h in heads]
    jobs = make_jobs(cfg)

    lhs3 = np.zeros((3, GS * 128), np.float32)
    for g in range(GS):
        a = np.float32(SQD * slopes[g] * 128.0)
        a_hi = np.float32(ml_dtypes.bfloat16(a))
        a_lo = np.float32(ml_dtypes.bfloat16(np.float32(a - a_hi)))
        lhs3[0, g * 128:(g + 1) * 128] = 1.0
        lhs3[1, g * 128:(g + 1) * 128] = a_hi
        lhs3[2, g * 128:(g + 1) * 128] = a_lo
    d["lhs3"] = _bf16(lhs3)

    qi = np.arange(128, dtype=np.float32)
    qib = np.zeros((1, GS * 512), np.float32)
    for g in range(GS):
        qib[0, g * 512:(g + 1) * 512] = np.tile(-SQD * slopes[g] * qi, 4)
    d["qib"] = qib.astype(np.float32)

    p = np.arange(128, dtype=np.float32)
    bt = np.zeros((128, max(len(jobs), 1)), np.float32)
    for ji, (g, T, ci) in enumerate(jobs):
        sl = slopes[g]
        bt[:, ji] = sl * p + sl * (512.0 * ci - 128.0 * T) + SHIFT
    d["biastab"] = bt

    dyn = np.zeros((1, 16), np.int32)
    for g in range(GS):
        hq = 8 * g + c
        dyn[0, g] = (hq // 4) * NC * 512 + (hq % 4) * 128
        hk = 2 * g + c // 4
        dyn[0, 4 + g] = (hk // 4) * NC * 512 + (hk % 4) * 128
        dyn[0, 9 + g] = hk * 128
    dyn[0, 8] = c * TOK
    d["dynoff"] = dyn
    return d


def build_shared_inputs(cfg, Wq, Wk, Wv, Wo, qnw, knw, ffk):
    DM = cfg.DM
    d = {}
    def _chunked(wt):
        DMC = DM // 128
        ncc = wt.shape[1] // 512
        return np.ascontiguousarray(
            wt.reshape(DMC, 128, ncc, 512).transpose(1, 2, 0, 3).reshape(128, ncc * DMC * 512))

    d["wqt"] = _bf16(_chunked(np.asarray(Wq, np.float32).T))
    d["wkt"] = _bf16(_chunked(np.asarray(Wk, np.float32).T))
    d["wvt"] = _bf16(_chunked(np.asarray(Wv, np.float32).T))
    WoT = np.ascontiguousarray(Wo.T, dtype=np.float32).reshape(H, HD, DM)
    perm = [r + 8 * g for r in range(NC) for g in range(GS)]
    wo2 = WoT[perm].reshape(H, 128, DM // 512, 512)
    d["wot"] = _bf16(np.ascontiguousarray(
        wo2.transpose(1, 2, 0, 3).reshape(128, (DM // 512) * H * 512)))
    Kf = np.asarray(ffk).reshape(cfg.RDIM, DM)
    d["kfh"] = np.ascontiguousarray(
        Kf.reshape(cfg.RC, 128, DM).transpose(1, 0, 2).reshape(128, cfg.RC * DM)).astype(np.int8)
    d["qnw"] = np.asarray(qnw, np.float32).reshape(1, HD)
    d["knw"] = np.asarray(knw, np.float32).reshape(1, HD)
    d["onesf"] = np.ones((1, 128), np.float32)
    cvh = np.zeros((128, 6), np.float32)
    cvh[:, 0] = EPS_LN; cvh[:, 1] = EPS_RMS; cvh[:, 2] = -0.5
    cvh[:, 3] = 1.0 / HD; cvh[:, 4] = INV; cvh[:, 5] = 1.0 / DM
    d["constv"] = cvh
    d["identb"] = np.eye(128, dtype=ml_dtypes.bfloat16)
    d["identf"] = np.eye(128, dtype=np.float32)
    jv = np.zeros((2, 512), np.float32)
    for j in range(4):
        jv[:, j * 128:(j + 1) * 128] = j
    d["jvec2"] = _bf16(jv)
    sel = np.tile(np.eye(128, dtype=np.float32), (1, 4))
    d["selq"] = _bf16(sel)
    masks = np.zeros((128, 4 * 512), np.uint8)
    p = np.arange(128, dtype=np.float32)
    qi = np.arange(128, dtype=np.float32)
    for ph in range(4):
        for j in range(4):
            srel = 128.0 * j + p[:, None]
            qrel = 128.0 * ph + qi[None, :]
            masks[:, ph * 512 + j * 128: ph * 512 + (j + 1) * 128] = (srel > qrel).astype(np.uint8)
    d["maskst"] = masks
    return d


_CFG = CFG()


def kernel(L0_attn_x, L0_attn_Wq, L0_attn_Wk, L0_attn_Wv, L0_attn_Wo,
           L0_attn_q_norm_w, L0_attn_k_norm_w, L0_ffn_kernel):
    cfg = _CFG
    x = np.ascontiguousarray(np.asarray(L0_attn_x, np.float32).reshape(cfg.S, cfg.DM))
    Wq = np.asarray(L0_attn_Wq, np.float32)
    Wk = np.asarray(L0_attn_Wk, np.float32)
    Wv = np.asarray(L0_attn_Wv, np.float32)
    Wo = np.asarray(L0_attn_Wo, np.float32)
    shared = build_shared_inputs(cfg, Wq, Wk, Wv, Wo, L0_attn_q_norm_w,
                                 L0_attn_k_norm_w, L0_ffn_kernel)
    nc = build_program(cfg)
    in_maps = [build_core_inputs(cfg, c, x, Wq, Wk, Wv, Wo, L0_attn_q_norm_w,
                                 L0_attn_k_norm_w, L0_ffn_kernel, shared)
               for c in range(NC)]
    res = run_bass_kernel_spmd(nc, in_maps, core_ids=list(range(NC)))
    out = np.concatenate([res.results[c]["yout"] for c in range(NC)], axis=0)
    return out.reshape(1, cfg.S, cfg.DM)


# version 17 (final)
# speedup vs baseline: 1.0213x; 1.0213x over previous
import math
import numpy as np
import ml_dtypes
import concourse.bass as bass
import concourse.mybir as mybir
from concourse.bass_utils import run_bass_kernel_spmd
from concourse.bass import ds
from contextlib import ExitStack

F32 = mybir.dt.float32
BF16 = mybir.dt.bfloat16
I8 = mybir.dt.int8
I32 = mybir.dt.int32
U8 = mybir.dt.uint8
AF = mybir.ActivationFunctionType
ALU = mybir.AluOpType
AX = mybir.AxisListType
ET = mybir.EngineType

NC = 8
WAVE = 4
H = 32
HKV = 8
HD = 128
GS = 4
HDTOT = H * HD
KVD = HKV * HD
SQD = math.sqrt(HD)
INV = 1.0 / SQD
EPS_RMS = 1e-6
EPS_LN = 1e-5
SHIFT = 45.0
T_WIN = 60.0


class CFG:
    def __init__(self, S=8192, DM=4096):
        self.S = S
        self.DM = DM
        self.TOK = S // NC
        self.NT = S // 128
        self.NTO = self.TOK // 128
        self.DMC = DM // 128
        self.NSUB = S // 128
        self.RDIM = DM // 8
        self.RC = self.RDIM // 128
        self.DC = DM // 512
        self.HC = HDTOT // 512
        self.KC = KVD // 512
        assert self.RDIM % 128 == 0 and DM % 512 == 0 and self.TOK % 128 == 0


def slope_of(h):
    return 2.0 ** (-0.25 * (h + 1))


def chunk_range(cfg, g, T):
    c_hi = (128 * T + 127) // 512
    wmax = T_WIN / slope_of(8 * g + 7)
    c_lo = max(0, math.ceil((128 * T - 384 - wmax) / 512.0))
    return min(c_lo, c_hi), c_hi


def make_jobs(cfg):
    jobs = []
    for g in range(GS):
        for T in range(cfg.NT):
            c_lo, c_hi = chunk_range(cfg, g, T)
            for ci in range(c_lo, c_hi + 1):
                jobs.append((g, T, ci))
    return jobs


class Rec:
    ENG = ("t", "v", "s", "p", "y")

    def __init__(self):
        self.ops = {e: [] for e in self.ENG}
        self.cnt = {}
        self.lastwait = {e: {} for e in self.ENG}

    def inc(self, sem, n=1):
        self.cnt[sem] = self.cnt.get(sem, 0) + n
        return self.cnt[sem]

    def val(self, sem):
        return self.cnt.get(sem, 0)

    def wait(self, eng, sem, v):
        if v is None or v <= 0:
            return
        if self.lastwait[eng].get(sem, 0) >= v:
            return
        self.lastwait[eng][sem] = v
        self.ops[eng].append(("w", sem, v))

    def emit(self, eng, fn, sem=None, n=1):
        if sem in ("V", "S", "P"):
            self.wait(eng, sem, self.val(sem))
        if sem is None:
            self.ops[eng].append(("o", fn, None, 0))
            return None
        v = self.inc(sem, n)
        self.ops[eng].append(("o", fn, sem, n))
        return v


def _replay(rec, key, eng, sems):
    for item in rec.ops[key]:
        if item[0] == "w":
            eng.wait_ge(sems[item[1]], item[2])
        else:
            inst = item[1](eng)
            if item[2] is not None:
                inst.then_inc(sems[item[2]], item[3])


def build_program(cfg, upto=9):
    nc = bass.Bass("TRN2", target_bir_lowering=False, debug=False)
    S, DM, TOK = cfg.S, cfg.DM, cfg.TOK
    NT, NTO, DMC, NSUB = cfg.NT, cfg.NTO, cfg.DMC, cfg.NSUB
    RDIM, RC, DC = cfg.RDIM, cfg.RC, cfg.DC
    jobs = make_jobs(cfg)
    NJOBS = len(jobs)

    xa = nc.dram_tensor("x", [TOK, DM], F32, kind="ExternalInput").ap()
    xta = nc.dram_tensor("xt", [128, DMC * TOK], BF16, kind="ExternalInput").ap()
    _CW = (DM // 128) * 512
    wqt = nc.dram_tensor("wqt", [128, (HDTOT // 512) * _CW], BF16, kind="ExternalInput").ap()
    wkt = nc.dram_tensor("wkt", [128, (KVD // 512) * _CW], BF16, kind="ExternalInput").ap()
    wvt = nc.dram_tensor("wvt", [128, (KVD // 512) * _CW], BF16, kind="ExternalInput").ap()
    wot = nc.dram_tensor("wot", [128, (DM // 512) * H * 512], BF16, kind="ExternalInput").ap()
    kfh = nc.dram_tensor("kfh", [128, RC * DM], I8, kind="ExternalInput").ap()
    qnw = nc.dram_tensor("qnw", [1, HD], F32, kind="ExternalInput").ap()
    knw = nc.dram_tensor("knw", [1, HD], F32, kind="ExternalInput").ap()
    onesf = nc.dram_tensor("onesf", [1, 128], F32, kind="ExternalInput").ap()
    constv = nc.dram_tensor("constv", [128, 6], F32, kind="ExternalInput").ap()
    identb = nc.dram_tensor("identb", [128, 128], BF16, kind="ExternalInput").ap()
    identf = nc.dram_tensor("identf", [128, 128], F32, kind="ExternalInput").ap()
    lhs3 = nc.dram_tensor("lhs3", [3, GS * 128], BF16, kind="ExternalInput").ap()
    jvec2 = nc.dram_tensor("jvec2", [2, 512], BF16, kind="ExternalInput").ap()
    qib = nc.dram_tensor("qib", [1, GS * 512], F32, kind="ExternalInput").ap()
    selq = nc.dram_tensor("selq", [128, 512], BF16, kind="ExternalInput").ap()
    biastab = nc.dram_tensor("biastab", [128, max(NJOBS, 1)], F32, kind="ExternalInput").ap()
    maskst = nc.dram_tensor("maskst", [128, 4 * 512], U8, kind="ExternalInput").ap()
    dynoff = nc.dram_tensor("dynoff", [1, 16], I32, kind="ExternalInput").ap()
    yout = nc.dram_tensor("yout", [TOK, DM], F32, kind="ExternalOutput").ap()

    kag_i = nc.dram_tensor("kag_i", [KVD, TOK], BF16)
    vag_i = nc.dram_tensor("vag_i", [TOK, KVD], BF16)
    qag_i = nc.dram_tensor("qag_i", [HDTOT, TOK], BF16)
    oag_i = nc.dram_tensor("oag_i", [GS * 128, S], BF16)
    kag_o = nc.dram_tensor("kag_o", [NC * KVD + 512, TOK], BF16, addr_space="Shared")
    vag_o = nc.dram_tensor("vag_o", [NC * TOK, KVD], BF16, addr_space="Shared")
    qag_o = nc.dram_tensor("qag_o", [NC * HDTOT + 512, TOK], BF16, addr_space="Shared")
    oag_o = nc.dram_tensor("oag_o", [GS * NC * 128, S], BF16, addr_space="Shared")
    attd = nc.dram_tensor("attd", [TOK, DM], F32)
    lnscr = nc.dram_tensor("lnscr", [TOK, DM], BF16)
    ktok = nc.dram_tensor("ktok", [TOK, KVD], BF16)
    qtok = nc.dram_tensor("qtok", [TOK, 1024], BF16)
    oscr = nc.dram_tensor("oscr", [2 * 128, 128], BF16)
    oag_l = nc.dram_tensor("oag_l", [GS * NC * 128, TOK], BF16)

    rec = Rec()
    st = {}
    dyn = {}

    def dma(q, fn, sem=None):
        if sem is None:
            sem = "dG" if q == "p" else "dS"
        return rec.emit(q, fn, sem, 16)

    ALLSEMS = ["dG", "dS", "CC", "T", "V", "S", "P",
               "dW0", "dW1", "dW2", "dW3", "dX0", "dX1", "dA0", "dA1", "dT0", "dT1",
               "dP0", "dP1", "dY0", "dY1", "dZ0", "dZ1"]

    def phase_barrier():
        for eng in ("t", "v", "s", "p", "y"):
            for sem in ALLSEMS:
                if sem != "CC":
                    rec.wait(eng, sem, rec.val(sem))

    with ExitStack() as stk:
        e = stk.enter_context
        sems = {nm: e(nc.semaphore(nm)) for nm in
                ["dG", "dS", "CC", "T", "V", "S", "P",
                 "dW0", "dW1", "dW2", "dW3", "dX0", "dX1", "dA0", "dA1", "dT0", "dT1",
                 "dP0", "dP1", "dY0", "dY1", "dZ0", "dZ1"]}

        arena = e(nc.sbuf_tensor("arena", [128, 164 * 1024], U8))
        _ESZ = {F32: 4, BF16: 2, U8: 1, I32: 4}

        def carve(off_kb, shape, dtype):
            off = int(off_kb * 1024)
            n = shape[1] * _ESZ[dtype]
            assert off % 512 == 0 and off + n <= 164 * 1024, (off, n)
            v = arena[:, off:off + n].bitcast(dtype)
            return v

        onesf_s = e(nc.sbuf_tensor("onesf_s", [1, 128], F32))
        identb_s = e(nc.sbuf_tensor("identb_s", [128, 128], BF16))
        identf_s = e(nc.sbuf_tensor("identf_s", [128, 128], F32))
        onesb = e(nc.sbuf_tensor("onesb", [1, 128], BF16))
        onescol = e(nc.sbuf_tensor("onescol", [128, 1], BF16))
        negmub = e(nc.sbuf_tensor("negmub", [128, 1], BF16))
        nmrow = e(nc.sbuf_tensor("nmrow", [1, NTO * 128], BF16))
        rstd5 = e(nc.sbuf_tensor("rstd5", [128, NTO], F32))
        cv = e(nc.sbuf_tensor("cv", [128, 6], F32))
        lhs3_s = e(nc.sbuf_tensor("lhs3_s", [3, GS * 128], BF16))
        qib_s = e(nc.sbuf_tensor("qib_s", [1, GS * 512], F32))
        btab_s = e(nc.sbuf_tensor("btab_s", [128, max(NJOBS, 1)], F32))
        masks_s = e(nc.sbuf_tensor("masks_s", [128, 4 * 512], U8))
        dyn_s = e(nc.sbuf_tensor("dyn_s", [1, 16], I32))
        qnw_s = e(nc.sbuf_tensor("qnw_s", [1, HD], F32))
        knw_s = e(nc.sbuf_tensor("knw_s", [1, HD], F32))
        wqk_s = e(nc.sbuf_tensor("wqk_s", [128, HD], F32))
        zero_b = e(nc.sbuf_tensor("zero_b", [128, 512], BF16))
        neg_f = e(nc.sbuf_tensor("neg_f", [128, 512], F32))
        stag = [e(nc.sbuf_tensor(f"stag{i}", [3, 512], BF16)) for i in range(2)]
        mrow = [e(nc.sbuf_tensor(f"mrow{i}", [1, 128], F32)) for i in range(2)]
        abuf = [e(nc.sbuf_tensor(f"abuf{i}", [128, 512], BF16)) for i in range(4)]
        rcp_s = e(nc.sbuf_tensor("rcp_s", [128, 1], F32))
        outh_b = e(nc.sbuf_tensor("outh_b", [128, 128], BF16))
        ln_sum = [e(nc.sbuf_tensor(f"ln_sum{i}", [128, 1], F32)) for i in range(2)]
        ln_ssq = [e(nc.sbuf_tensor(f"ln_ssq{i}", [128, 1], F32)) for i in range(2)]
        ln_mean = [e(nc.sbuf_tensor(f"ln_mean{i}", [128, 1], F32)) for i in range(2)]
        ln_var = [e(nc.sbuf_tensor(f"ln_var{i}", [128, 1], F32)) for i in range(2)]
        ln_rstd = [e(nc.sbuf_tensor(f"ln_rstd{i}", [128, 1], F32)) for i in range(2)]
        ln_t1 = [e(nc.sbuf_tensor(f"ln_t1{i}", [128, 1], F32)) for i in range(2)]
        SQW = max(DM // 2, 2048)
        sq_scr = e(nc.sbuf_tensor("sq_scr", [128, SQW], F32))
        ln_h2 = [e(nc.sbuf_tensor(f"ln_h2{i}", [128, 1], F32)) for i in range(2)]
        rms_ssq = [e(nc.sbuf_tensor(f"rms_ssq{i}", [128, 4], F32)) for i in range(4)]
        rms_rstd = [e(nc.sbuf_tensor(f"rms_rstd{i}", [128, 4], F32)) for i in range(4)]

        dma("p", lambda g: g.dma_start(out=onesf_s[:, :], in_=onesf))
        dma("p", lambda g: g.dma_start(out=identb_s[:, :], in_=identb))
        dma("p", lambda g: g.dma_start(out=identf_s[:, :], in_=identf))
        dma("p", lambda g: g.dma_start(out=cv[:, :], in_=constv))
        dma("p", lambda g: g.dma_start(out=lhs3_s[:, :], in_=lhs3))
        dma("p", lambda g: g.dma_start(out=qib_s[:, :], in_=qib))
        dma("p", lambda g: g.dma_start(out=btab_s[:, :], in_=biastab))
        dma("p", lambda g: g.dma_start(out=masks_s[:, :], in_=maskst))
        dma("p", lambda g: g.dma_start(out=dyn_s[:, :], in_=dynoff))
        dma("p", lambda g: g.dma_start(out=qnw_s[:, :], in_=qnw))
        dma("p", lambda g: g.dma_start(out=knw_s[:, :], in_=knw))
        for i in range(2):
            dma("p", lambda g, i=i: g.dma_start(out=stag[i][1:3, :], in_=jvec2))
        ld0 = rec.val("dG")

        rec.wait("p", "dG", ld0)

        def load_dyn(g):
            mx = {"q": NC * HDTOT - NC * 512 + 512, "k": NC * KVD - NC * 512 + 512,
                  "o": (NC - 1) * TOK, "v": KVD - 128}
            for i, nm in enumerate(["q0", "q1", "q2", "q3", "k0", "k1", "k2", "k3", "oo",
                                    "v0", "v1", "v2", "v3"]):
                dyn[nm] = nc.values_load(dyn_s[0:1, i:i + 1], engines=[ET.Pool],
                                         min_val=0, max_val=mx[nm[0]],
                                         skip_runtime_bounds_check=True)
            return g.memset(rcp_s[:, :], 0.0)
        rec.emit("p", load_dyn, "P")
        rec.wait("y", "dG", ld0)

        def load_dyn_sp(y):
            mxq = NC * HDTOT - NC * 512 + 512
            for i in range(4):
                dyn["q%d_sp" % i] = nc.values_load(dyn_s[0:1, i:i + 1], engines=[ET.SP],
                                                   min_val=0, max_val=mxq,
                                                   skip_runtime_bounds_check=True)
            for i in range(4):
                dyn["vy%d" % i] = nc.values_load(dyn_s[0:1, 9 + i:10 + i], engines=[ET.SP],
                                                 min_val=0, max_val=KVD - 128,
                                                 skip_runtime_bounds_check=True)
            return None
        rec.emit("y", load_dyn_sp, None)

        PB = [e(nc.psum_tensor(f"PB{i}", [128, 512], F32)) for i in range(8)]
        pST_pre = PB[2:4]
        PBb = [PB[6].bitcast(BF16), PB[7].bitcast(BF16)]

        lnxT = carve(0, [128, DMC * TOK], BF16)
        kvsl = carve(64, [128, NTO * KVD], BF16)
        qnsl = carve(80, [128, 2 * NTO * 512], BF16)
        ktmp = [carve(96, [128, 512], BF16), carve(97, [128, 512], BF16),
                carve(162, [128, 512], BF16), carve(163, [128, 512], BF16)]
        p01 = PB[0:6]
        p01b = PBb

        rec.wait("v", "dG", ld0)
        rec.emit("v", lambda v: v.memset(zero_b[:, :], 0.0), "V")
        rec.emit("v", lambda v: v.memset(neg_f[:, :], -1.0e6), "V")
        rec.emit("v", lambda v: v.tensor_tensor(qnw_s[0:1, :], qnw_s[0:1, :], knw_s[0:1, :], ALU.mult), "V")
        vwqk = rec.val("V")
        rec.emit("v", lambda v: v.tensor_reduce(mrow[0][0:1, 0:1], qnw_s[0:1, :], AX.X, ALU.max), "V")
        rec.emit("v", lambda v: v.tensor_reduce(mrow[0][0:1, 1:2], qnw_s[0:1, :], AX.X, ALU.min), "V")
        rec.emit("v", lambda v: v.tensor_scalar(mrow[0][0:1, 1:2], mrow[0][0:1, 1:2], -1.0, 0.0, ALU.mult, ALU.add), "V")
        rec.emit("v", lambda v: v.tensor_tensor(mrow[0][0:1, 0:1], mrow[0][0:1, 0:1], mrow[0][0:1, 1:2], ALU.max), "V")
        rec.emit("v", lambda v: v.tensor_scalar(mrow[1][0:1, 0:1], mrow[0][0:1, 0:1], -float(HD), 0.0, ALU.mult, ALU.add), "V")
        rec.emit("v", lambda v: v.tensor_scalar(qib_s[0:1, :], qib_s[0:1, :], mrow[1][0:1, 0:1], 0.0, ALU.add, ALU.add), "V")
        qshift_v = rec.val("V")
        rec.wait("t", "V", vwqk)
        rec.wait("t", "dG", ld0)
        rec.emit("t", lambda t: t.matmul(p01[0][:, 0:HD], onesf_s[0:1, :], qnw_s[0:1, :], start=True, stop=True), "T")
        twqk = rec.val("T")
        rec.wait("s", "T", twqk)
        rec.emit("s", lambda s: s.copy(wqk_s[:, :], p01[0][:, 0:HD]), "S")
        swqk = rec.val("S")
        st[("evS", 0)] = swqk

        def dummy_out():
            rec.wait("v", "dG", rec.val("dG"))
            rec.emit("v", lambda v: v.memset(sq_scr[:, :], 0.0), "V")
            dv = rec.val("V")
            rec.wait("y", "V", dv)
            for t_ in range(NTO):
                for off in range(0, DM, SQW):
                    w_ = min(SQW, DM - off)
                    dma("y", lambda g, t_=t_, off=off, w_=w_: g.dma_start(
                        out=yout[t_ * 128:(t_ + 1) * 128, off:off + w_], in_=sq_scr[:, 0:w_]))
            rec.wait("y", "dS", rec.val("dS"))
            rec.wait("p", "dG", rec.val("dG"))
            block = e(nc.Block())
            block.gpsimd(lambda g: _replay(rec, "p", g, sems))
            block.sync(lambda y: _replay(rec, "y", y, sems))
            block.tensor(lambda t: _replay(rec, "t", t, sems))
            block.scalar(lambda s: _replay(rec, "s", s, sems))
            block.vector(lambda v: _replay(rec, "v", v, sems))

        if upto == 0:
            dummy_out()
            return nc

        def ln_stats(getbuf, pp=0):
            rec.wait("v", "S", st.get(("lnS", pp), 0))
            rec.wait("s", "V", st.get(("lnV", pp), 0))
            rec.emit("v", lambda v, pp=pp: v.tensor_reduce(ln_sum[pp][:, :], getbuf()[:, :], AX.X, ALU.add), "V")
            DM2 = DM // 2
            rec.emit("s", lambda s, pp=pp: s.activation(sq_scr[:, 0:DM2], getbuf()[:, 0:DM2], AF.Square, accum_out=ln_ssq[pp][:, :]), "S")
            rec.emit("s", lambda s, pp=pp: s.activation(sq_scr[:, 0:DM2], getbuf()[:, DM2:DM], AF.Square, accum_out=ln_h2[pp][:, :]), "S")
            ss = rec.val("S")
            rec.wait("v", "S", ss)
            rec.emit("v", lambda v, pp=pp: v.tensor_tensor(ln_ssq[pp][:, :], ln_ssq[pp][:, :], ln_h2[pp][:, :], ALU.add), "V")
            rec.emit("v", lambda v, pp=pp: v.tensor_scalar(ln_mean[pp][:, :], ln_sum[pp][:, :], 1.0 / DM, 0.0, ALU.mult, ALU.add), "V")
            rec.emit("v", lambda v, pp=pp: v.tensor_tensor(ln_t1[pp][:, :], ln_mean[pp][:, :], ln_mean[pp][:, :], ALU.mult), "V")
            rec.emit("v", lambda v, pp=pp: v.tensor_scalar(ln_var[pp][:, :], ln_ssq[pp][:, :], 1.0 / DM, 0.0, ALU.mult, ALU.add), "V")
            rec.emit("v", lambda v, pp=pp: v.tensor_tensor(ln_var[pp][:, :], ln_var[pp][:, :], ln_t1[pp][:, :], ALU.subtract), "V")
            vv = rec.val("V")
            st[("lnV", pp)] = vv
            rec.wait("s", "V", vv)
            rec.emit("s", lambda s, pp=pp: s.activation(ln_t1[pp][:, :], ln_var[pp][:, :], AF.Ln, bias=cv[:, 0:1]), "S")
            rec.emit("s", lambda s, pp=pp: s.activation(ln_rstd[pp][:, :], ln_t1[pp][:, :], AF.Exp, scale=cv[:, 2:3]), "S")
            st[("lnS", pp)] = rec.val("S")
            return rec.val("S")

        dma("y", lambda g: g.dma_start(out=lnxT[:, :], in_=xta), "dX0")
        xtv = rec.val("dX0")
        rec.emit("v", lambda v: v.memset(onesb[:, :], 1.0), "V")
        rec.emit("v", lambda v: v.memset(onescol[:, :], 1.0), "V")
        vmem = rec.val("V")
        rec.wait("t", "dX0", xtv)
        rec.wait("t", "V", vmem)
        for t in range(NTO):
            rec.wait("t", "V", st.get("p0sumV", 0))
            for k in range(DMC):
                rec.emit("t", lambda tt, k=k, t=t: tt.matmul(
                    PB[6][:, 0:1], lnxT[:, k * TOK + t * 128:k * TOK + (t + 1) * 128],
                    onescol[:, :], start=(k == 0), stop=(k == DMC - 1)), "T")
            tsv = rec.val("T")
            rec.wait("v", "T", tsv)
            rec.emit("v", lambda v: v.tensor_scalar(negmub[:, :], PB[6][:, 0:1], -1.0 / DM, 0.0, ALU.mult, ALU.add), "V")
            st["p0sumV"] = rec.val("V")
            rec.wait("t", "V", st["p0sumV"])
            rec.wait("t", "V", st.get("p0rowV", 0))
            rec.emit("t", lambda tt: tt.matmul(PB[7][0:1, 0:128], negmub[:, :], identb_s[:, :], start=True, stop=True), "T")
            trv = rec.val("T")
            rec.wait("v", "T", trv)
            rec.emit("v", lambda v, t=t: v.tensor_copy(nmrow[0:1, t * 128:(t + 1) * 128], PB[7][0:1, 0:128]), "V")
            st["p0rowV"] = rec.val("V")
        rec.wait("t", "V", st["p0rowV"])
        CG = min(512, TOK)
        NH = TOK // CG
        for k in range(DMC):
            for h in range(NH):
                pb_ = pST_pre[(k * NH + h) % 2]
                rec.wait("t", "V", st.get(("p0c", (k * NH + h) % 2), 0))
                rec.emit("t", lambda tt, pb_=pb_, h=h, CG=CG: tt.matmul(
                    pb_[:, 0:CG], onesb[:, :], nmrow[0:1, h * CG:(h + 1) * CG], start=True, stop=True), "T")
                tcv = rec.val("T")
                rec.wait("v", "T", tcv)
                rec.emit("v", lambda v, pb_=pb_, k=k, h=h, CG=CG: v.tensor_tensor(
                    lnxT[:, k * TOK + h * CG:k * TOK + (h + 1) * CG],
                    lnxT[:, k * TOK + h * CG:k * TOK + (h + 1) * CG], pb_[:, 0:CG], ALU.add), "V")
                st[("p0c", (k * NH + h) % 2)] = rec.val("V")
        rec.wait("t", "V", rec.val("V"))
        for t in range(NTO):
            rec.wait("t", "V", st.get("p0gV", 0))
            for k in range(DMC):
                rec.emit("t", lambda tt, k=k, t=t: tt.matmul(
                    PB[6][:, 128:256], lnxT[:, k * TOK + t * 128:k * TOK + (t + 1) * 128],
                    lnxT[:, k * TOK + t * 128:k * TOK + (t + 1) * 128],
                    start=(k == 0), stop=(k == DMC - 1)), "T")
            tgv = rec.val("T")
            rec.wait("v", "T", tgv)
            rec.emit("v", lambda v: v.tensor_tensor(sq_scr[:, 0:128], PB[6][:, 128:256], identf_s[:, :], ALU.mult), "V")
            rec.emit("v", lambda v, t=t: v.tensor_reduce(rstd5[:, t:t + 1], sq_scr[:, 0:128], AX.X, ALU.add), "V")
            st["p0gV"] = rec.val("V")
            rec.wait("s", "V", st["p0gV"])
            rec.emit("s", lambda s, t=t: s.activation(rstd5[:, t:t + 1], rstd5[:, t:t + 1], AF.Ln, bias=cv[:, 0:1], scale=cv[:, 5:6]), "S")
            rec.emit("s", lambda s, t=t: s.activation(rstd5[:, t:t + 1], rstd5[:, t:t + 1], AF.Exp, scale=cv[:, 2:3]), "S")
        p0S = rec.val("S")
        p0V = rec.val("V")
        rec.wait("v", "S", p0S)
        rec.wait("s", "V", p0V)
        rec.wait("t", "V", p0V)
        p0rd = [rec.val("dT0"), rec.val("dT1")]
        for i_ in range(6):
            st[("evV", i_)] = p0V
        wch = [carve(98, [128, DMC * 512], BF16), carve(130, [128, DMC * 512], BF16)]

        if upto == 1:
            dummy_out()
            return nc
        wsem_i = {"i": 0}

        def proj_prefetch(wt_ap, cc):
            wb = cc % 2
            wsem = "dW%d" % (wsem_i["i"] % 4)
            wsem_i["i"] += 1
            rec.wait("p", "T", st.get(("wchfree", wb), 0))
            dma("p", lambda g, wb=wb, cc=cc, wt_ap=wt_ap: g.dma_start(
                out=wch[wb][:, 0:DMC * 512],
                in_=wt_ap[:, cc * DMC * 512:(cc + 1) * DMC * 512]), wsem)
            return (wsem, rec.val(wsem), wb)

        def proj_phase(wt_ap, ncols, out_cb, name, chunk_done=None, pf0=None, chunk_pre=None):
            ncc = ncols // 512
            for cc in range(ncc):
                if cc == 0 and pf0 is not None:
                    wsem, dwv, wb = pf0
                else:
                    wsem, dwv, wb = proj_prefetch(wt_ap, cc)
                if chunk_pre is not None and cc > 1:
                    chunk_pre(cc - 2)
                for wave in range(0, NTO, WAVE):
                    tiles = list(range(wave, min(wave + WAVE, NTO)))
                    for k in range(DMC):
                        for t in tiles:
                            pb = p01[t % 6]
                            rec.wait("t", wsem, dwv)
                            rec.wait("t", "dT0", p0rd[0])
                            rec.wait("t", "dT1", p0rd[1])
                            rec.wait("t", "V", st.get(("evV", t % 6), 0))
                            rec.wait("t", "S", st.get(("evS", t % 6), 0))
                            rec.emit("t", lambda tt, pb=pb, k=k, t=t, wb=wb: tt.matmul(
                                pb[:, :], lnxT[:, k * TOK + t * 128: k * TOK + (t + 1) * 128],
                                wch[wb][:, k * 512:(k + 1) * 512],
                                start=(k == 0), stop=(k == DMC - 1)), "T")
                    st[("wchfree", wb)] = rec.val("T")
                    tmm = rec.val("T")
                    for t in tiles:
                        out_cb(t, cc, p01[t % 6], tmm)
                        st[("evV", t % 6)] = rec.val("V")
                        st[("evS", t % 6)] = rec.val("S")
                if chunk_done is not None:
                    chunk_done(cc)
            if chunk_pre is not None:
                return [cc_ for cc_ in range(max(0, ncc - 2), ncc)]
            return []

        def rms_evac(pb, tmm, pr, t):
            rec.wait("s", "T", tmm)
            rec.wait("s", "V", st.get(("rmsV", pr), 0))
            rec.emit("s", lambda s, pb=pb, pr=pr: s.activation(sq_scr[:, pr * 512:pr * 512 + 512], pb[:, :], AF.Square), "S")
            sv_ = rec.val("S")
            rec.wait("v", "S", sv_)
            rec.emit("v", lambda v, pr=pr: v.tensor_reduce(rms_ssq[pr][:, :], sq_scr[:, pr * 512:pr * 512 + 512].rearrange("p (h d) -> p h d", d=128), AX.X, ALU.add), "V")
            rec.emit("v", lambda v, pr=pr, t=t: v.tensor_scalar(rms_ssq[pr][:, :], rms_ssq[pr][:, :], rstd5[:, t:t + 1], rstd5[:, t:t + 1], ALU.mult, ALU.mult), "V")
            vr = rec.val("V")
            rec.wait("s", "V", vr)
            rec.emit("s", lambda s, pr=pr: s.activation(rms_ssq[pr][:, :], rms_ssq[pr][:, :], AF.Ln, bias=cv[:, 1:2], scale=cv[:, 3:4]), "S")
            rec.emit("s", lambda s, pr=pr: s.activation(rms_rstd[pr][:, :], rms_ssq[pr][:, :], AF.Exp, scale=cv[:, 2:3]), "S")
            sr = rec.val("S")
            rec.wait("v", "S", sr)
            rec.emit("v", lambda v, pr=pr, t=t: v.tensor_scalar(rms_rstd[pr][:, :], rms_rstd[pr][:, :], rstd5[:, t:t + 1], 0.0, ALU.mult, ALU.add), "V")

        KQSEM = ["dX0", "dX1", "dY0", "dY1"]

        def k_evac(t, cc, pb, tmm):
            par = t % 4
            px = t % 2
            rms_evac(pb, tmm, par, t)
            rec.wait("v", KQSEM[par], st.get(("ktmp_st", par), 0))
            for h in range(4):
                rec.emit("v", lambda v, pb=pb, h=h, par=par: v.scalar_tensor_tensor(
                    ktmp[par][:, h * 128:(h + 1) * 128],
                    pb[:, h * 128:(h + 1) * 128], rms_rstd[par][:, h:h + 1], wqk_s[:, :],
                    ALU.mult, ALU.mult), "V")
            kvv = rec.val("V")
            st[("rmsV", par)] = kvv
            rec.wait("y", "V", kvv)
            dma("y", lambda g, t=t, cc=cc, par=par: g.dma_start(
                out=ktok[t * 128:(t + 1) * 128, cc * 512:(cc + 1) * 512], in_=ktmp[par][:, :]), KQSEM[par])
            st[("ktmp_st", par)] = rec.val(KQSEM[par])

        kpub_box = {}

        def v_evac(t, cc, pb, tmm):
            rec.wait("v", "T", tmm)
            for kc in range(KVD // 512):
                rec.wait("v", kpub_box[kc][0], kpub_box[kc][1])
            rec.emit("v", lambda v, pb=pb, t=t, cc=cc: v.tensor_scalar(
                kvsl[:, t * KVD + cc * 512: t * KVD + (cc + 1) * 512], pb[:, :],
                rstd5[:, t:t + 1], 0.0, ALU.mult, ALU.add), "V")

        def q_evac(t, cc, pb, tmm):
            par = t % 4
            px = t % 2
            rms_evac(pb, tmm, par, t)
            rec.wait("v", KQSEM[par], st.get(("ktmp_st", par), 0))
            for h in range(4):
                rec.emit("v", lambda v, pb=pb, h=h, par=par: v.tensor_scalar(
                    ktmp[par][:, h * 128:(h + 1) * 128],
                    pb[:, h * 128:(h + 1) * 128], rms_rstd[par][:, h:h + 1], 0.0, ALU.mult, ALU.add), "V")
            kvv = rec.val("V")
            st[("rmsV", par)] = kvv
            rec.wait("y", "V", kvv)
            rec.wait("y", "dT1", st.get(("qtok_rd", cc % 2), 0))
            dma("y", lambda g, t=t, cc=cc, par=par: g.dma_start(
                out=qtok[t * 128:(t + 1) * 128, (cc % 2) * 512:(cc % 2) * 512 + 512], in_=ktmp[par][:, :]), KQSEM[par])
            st[("ktmp_st", par)] = rec.val(KQSEM[par])

        def k_pub_chunk(cc):
            for sm in KQSEM:
                rec.wait("y", sm, rec.val(sm))
            for kv in range(4 * cc, 4 * cc + 4):
                for t in range(NTO):
                    dma("y", lambda g, kv=kv, t=t: g.dma_start(
                        out=kvsl[:, kv * TOK + t * 128: kv * TOK + (t + 1) * 128],
                        in_=ktok[t * 128:(t + 1) * 128, kv * 128:(kv + 1) * 128], transpose=True), "dT0")
            rec.wait("y", "dT0", rec.val("dT0"))
            dma("y", lambda g, cc=cc: g.dma_start(
                out=kag_i[cc * 512:(cc + 1) * 512, :].rearrange("(kv p) t -> p kv t", p=128),
                in_=kvsl[:, 4 * cc * TOK:(4 * cc + 4) * TOK].rearrange("p (kv t) -> p kv t", t=TOK)),
                "dZ%d" % (cc % 2))
            kpub_box[cc] = ("dZ%d" % (cc % 2), rec.val("dZ%d" % (cc % 2)))

        def k_chunk_ag(cc):
            rec.wait("p", kpub_box[cc][0], kpub_box[cc][1])
            rec.emit("p", lambda g, cc=cc: g.collective_compute(
                "AllGather", ALU.bypass, replica_groups=[list(range(NC))],
                ins=[kag_i[cc * 512:(cc + 1) * 512, :].opt()],
                outs=[kag_o[cc * NC * 512:(cc + 1) * NC * 512, :].opt()]), "CC")

        pend_k = proj_phase(wkt, KVD, k_evac, "K", chunk_done=k_pub_chunk, chunk_pre=k_chunk_ag)
        pf_q = proj_prefetch(wqt, 0)
        for cc in pend_k:
            k_chunk_ag(cc)
        cc_k = rec.val("CC")

        def q_publish(cc):
            half = (cc % 2) * NTO * 512
            for sm in KQSEM:
                rec.wait("y", sm, rec.val(sm))
            rec.wait("y", "dP%d" % (cc % 2), rec.val("dP%d" % (cc % 2)))
            for h in range(4):
                for t in range(NTO):
                    dma("y", lambda g, h=h, t=t, cc=cc, half=half: g.dma_start(
                        out=qnsl[:, half + h * TOK + t * 128: half + h * TOK + (t + 1) * 128],
                        in_=qtok[t * 128:(t + 1) * 128, (cc % 2) * 512 + h * 128:(cc % 2) * 512 + (h + 1) * 128],
                        transpose=True), "dT1")
            rdv = rec.val("dT1")
            st[("qtok_rd", cc % 2)] = rdv
            rec.wait("y", "dT1", rdv)
            qsem = "dP%d" % (cc % 2)
            dma("y", lambda g, cc=cc, half=half: g.dma_start(
                out=qag_i[cc * 512:(cc + 1) * 512, :].rearrange("(hl p) t -> p hl t", p=128),
                in_=qnsl[:, half:half + NTO * 512].rearrange("p (hl t) -> p hl t", t=TOK)), qsem)
            st[("qpub", cc % 2)] = (qsem, rec.val(qsem))

        def q_chunk_ag(cc):
            qsem, qv = st[("qpub", cc % 2)]
            rec.wait("p", qsem, qv)
            rec.emit("p", lambda g, cc=cc: g.collective_compute(
                "AllGather", ALU.bypass, replica_groups=[list(range(NC))],
                ins=[qag_i[cc * 512:(cc + 1) * 512, :].opt()],
                outs=[qag_o[cc * NC * 512:(cc + 1) * NC * 512, :].opt()]), "CC")

        pend_q = proj_phase(wqt, HDTOT, q_evac, "Q", chunk_done=q_publish, pf0=pf_q, chunk_pre=q_chunk_ag)
        pf_v = proj_prefetch(wvt, 0)
        for cc in pend_q:
            q_chunk_ag(cc)
        cc_q = rec.val("CC")

        proj_phase(wvt, KVD, v_evac, "V", pf0=pf_v)
        vS = rec.val("V")
        rec.wait("y", "V", vS)
        dma("y", lambda g: g.dma_start(
            out=vag_i.rearrange("(t p) k -> p t k", p=128),
            in_=kvsl[:, :].rearrange("p (t k) -> p t k", k=KVD)))
        vpub = rec.val("dS")
        rec.wait("p", "dS", vpub)
        rec.emit("p", lambda g: g.collective_compute(
            "AllGather", ALU.bypass, replica_groups=[list(range(NC))],
            ins=[vag_i.ap().opt()], outs=[vag_o.ap().opt()]), "CC")
        cc_v = rec.val("CC")

        if upto == 2:
            rec.wait("p", "CC", rec.val("CC"))
            dummy_out()
            return nc
        phase_barrier()

        KSZ = S * 2 // 1024
        ksb = [carve(0, [128, S], BF16), carve(KSZ, [128, S], BF16)]
        vsb = [carve(2 * KSZ, [128, NSUB * 129], BF16),
               carve(2 * KSZ + (NSUB * 129 * 2 + 1023) // 1024 + 1, [128, NSUB * 129], BF16)]
        _v2 = 2 * KSZ + 2 * ((NSUB * 129 * 2 + 1023) // 1024 + 1)
        qsb = [carve(_v2, [128, S], BF16), carve(_v2 + KSZ, [128, S], BF16)]
        otg = carve(_v2 + 2 * KSZ, [128, S], BF16)
        pST = PB[0:4]
        pO_full = PB[4:6]
        pO_ = [t[:, 0:129] for t in pO_full]
        pTRb = PBb

        rec.emit("v", lambda v: v.memset(vsb[0][:, :], 1.0), "V")
        rec.emit("v", lambda v: v.memset(vsb[1][:, :], 1.0), "V")
        vones = rec.val("V")

        def gather_g(g_idx, bp):
            asem = "dA%d" % bp
            qsem = "dP%d" % bp
            tfree = rec.val("T")
            rec.wait("p", "T", tfree)
            rec.wait("p", "CC", cc_k)
            dma("p", lambda g, g_idx=g_idx, bp=bp: g.dma_start(
                out=ksb[bp][:, :].rearrange("p (r t) -> p r t", t=TOK),
                in_=kag_o.ap()[ds(dyn["k%d" % g_idx], NC * 512), :].rearrange(
                    "(j w) t -> w j t", w=512)[0:128, :, :]), asem)
            rec.wait("y", "T", tfree)
            rec.wait("y", "CC", cc_q)
            dma("y", lambda g, g_idx=g_idx, bp=bp: g.dma_start(
                out=qsb[bp][:, :].rearrange("p (r t) -> p r t", t=TOK),
                in_=qag_o.ap()[ds(dyn["q%d_sp" % g_idx], NC * 512), :].rearrange(
                    "(j w) t -> w j t", w=512)[0:128, :, :]), qsem)
            v_k = rec.val(asem)
            v_q = rec.val(qsem)
            vsem = "dW%d" % (2 + bp)
            zsem = "dZ%d" % bp
            NJH = (NSUB // 2) if NSUB >= 2 else NSUB
            rec.wait("p", "V", vones)
            rec.wait("p", "CC", cc_v)
            dma("p", lambda g, g_idx=g_idx, bp=bp, NJH=NJH: g.dma_start(
                out=vsb[bp][:, 0:NJH * 129].rearrange("p (jj c) -> p jj c", c=129)[:, :, 0:128],
                in_=vag_o[0:NJH * 128, :][:, ds(dyn["v%d" % g_idx], 128)].rearrange(
                    "(jj p) d -> p jj d", p=128)), vsem)
            if NJH < NSUB:
                rec.wait("y", "V", vones)
                rec.wait("y", "T", tfree)
                rec.wait("y", "CC", cc_v)
                dma("y", lambda g, g_idx=g_idx, bp=bp, NJH=NJH: g.dma_start(
                    out=vsb[bp][:, NJH * 129:].rearrange("p (jj c) -> p jj c", c=129)[:, :, 0:128],
                    in_=vag_o[NJH * 128:, :][:, ds(dyn["vy%d" % g_idx], 128)].rearrange(
                        "(jj p) d -> p jj d", p=128)), zsem)
            return (asem, v_k, vsem, rec.val(vsem), qsem, v_q, zsem, rec.val(zsem))

        rec.wait("t", "dG", ld0)
        job_i = {"i": 0}
        NPST = 4

        def emit_preST(g_idx, T, par, bp, stg_v, slot, ci, gath_v):
            pst = pST[slot]
            ji = job_i["i"]
            job_i["i"] += 1
            rec.wait("t", gath_v[0], gath_v[1])
            rec.wait("t", gath_v[4], gath_v[5])
            rec.wait("t", "V", stg_v)
            rec.wait("t", "S", st.get(("stS", slot), 0))
            rec.wait("t", "V", st.get(("pst_busyV", slot), 0))
            rec.emit("t", lambda tt, pst=pst, g_idx=g_idx, par=par: tt.matmul(
                pst[:, :], lhs3_s[:, g_idx * 128:(g_idx + 1) * 128], stag[par][:, :],
                start=True, stop=False), "T")
            for j in range(4):
                rec.emit("t", lambda tt, pst=pst, ci=ci, j=j, T=T, bp=bp: tt.matmul(
                    pst[:, j * 128:(j + 1) * 128],
                    ksb[bp][:, ci * 512 + j * 128: ci * 512 + (j + 1) * 128],
                    qsb[bp][:, T * 128:(T + 1) * 128], start=False, stop=(j == 3)), "T")
            tmm = rec.val("T")
            c_lo, c_hi = chunk_range(cfg, g_idx, T)
            if ci == c_hi:
                phase = T % 4
                rec.wait("v", "T", tmm)
                rec.emit("v", lambda v, pst=pst, phase=phase: v.copy_predicated(
                    pst[:, :], masks_s[:, phase * 512:(phase + 1) * 512], neg_f[:, :]), "V")
                vmask = rec.val("V")
                rec.wait("s", "V", vmask)
                st[("pst_busyV", slot)] = vmask
            ab = abuf[slot]
            rec.wait("s", "T", tmm)
            rec.wait("s", "T", st.get(("ab_busyT", slot), 0))
            rec.emit("s", lambda s, ab=ab, pst=pst, ji=ji: s.activation(
                ab[:, :], pst[:, :], AF.Exp, bias=btab_s[:, ji:ji + 1], scale=cv[:, 4:5]), "S")
            st[("stS", slot)] = rec.val("S")
            return rec.val("S")

        def evac_T(T, par):
            po = pO_[par]
            tdone = rec.val("T")
            rec.wait("v", "T", tdone)
            ob = st.get("outh_busy_dX")
            if ob is not None:
                rec.wait("v", "dX%d" % ob[0], ob[1])
            rec.wait("v", "dS", st.get("otg_busy", 0))
            rec.emit("v", lambda v, po=po: v.reciprocal(rcp_s[:, :], po[:, 128:129]), "V")
            rec.emit("v", lambda v, po=po: v.tensor_scalar(outh_b[:, :], po[:, 0:128], rcp_s[:, :], 0.0, ALU.mult, ALU.add), "V")
            vev = rec.val("V")
            st[("ofree", par)] = vev
            slot = T % 2
            rec.wait("y", "V", vev)
            rec.wait("y", "dT%d" % slot, st.get(("oscr_rd", slot), 0))
            dma("y", lambda g, slot=slot: g.dma_start(
                out=oscr[slot * 128:(slot + 1) * 128, :], in_=outh_b[:, :]), "dX%d" % slot)
            stv = rec.val("dX%d" % slot)
            st["outh_busy_dX"] = (slot, stv)
            rec.wait("y", "dX%d" % slot, stv)
            dma("y", lambda g, slot=slot, T=T: g.dma_start(
                out=otg[:, T * 128:(T + 1) * 128],
                in_=oscr[slot * 128:(slot + 1) * 128, :], transpose=True), "dT%d" % slot)
            st[("oscr_rd", slot)] = rec.val("dT%d" % slot)

        def emit_pv(jb, bp, gvv):
            (T, par, ci, first, last, slot, expv) = jb
            po = pO_[par]
            rec.wait("t", gvv[2], gvv[3])
            rec.wait("t", gvv[6], gvv[7])
            if first:
                rec.wait("t", "V", st.get(("ofree", par), 0))
            rec.wait("t", "S", expv)
            ab = abuf[slot]
            for j in range(4):
                rec.emit("t", lambda tt, po=po, ab=ab, ci=ci, j=j, bp=bp, first=first, last=last: tt.matmul(
                    po[:, :], ab[:, j * 128:(j + 1) * 128],
                    vsb[bp][:, (4 * ci + j) * 129:(4 * ci + j) * 129 + 129],
                    start=(first and j == 0), stop=(last and j == 3)), "T")
            st[("ab_busyT", slot)] = rec.val("T")
            if last:
                evac_T(T, par)

        gv = {0: gather_g(0, 0)}
        for g_idx in range(GS):
            bp = g_idx % 2
            if g_idx + 1 < GS:
                gv[g_idx + 1] = gather_g(g_idx + 1, 1 - bp)
            rec.wait("v", "T", st.get("stag_busyT", 0))
            rec.wait("v", "V", qshift_v)
            for par in range(2):
                rec.emit("v", lambda v, par=par, g_idx=g_idx: v.tensor_copy(
                    stag[par][0:1, :], qib_s[0:1, g_idx * 512:(g_idx + 1) * 512]), "V")
            stg_v = rec.val("V")
            jn = 0
            prev = None
            for T in range(NT):
                par = T % 2
                c_lo, c_hi = chunk_range(cfg, g_idx, T)
                for ci in range(c_lo, c_hi + 1):
                    slot = jn % NPST
                    expv = emit_preST(g_idx, T, par, bp, stg_v, slot, ci, gv[g_idx])
                    jb = (T, par, ci, ci == c_lo, ci == c_hi, slot, expv)
                    if prev is not None:
                        emit_pv(prev, bp, gv[g_idx])
                    prev = jb
                    jn += 1
            emit_pv(prev, bp, gv[g_idx])
            st["stag_busyT"] = rec.val("T")
            st["ksb_free_%d" % bp] = rec.val("T")
            rec.wait("y", "dT0", rec.val("dT0"))
            rec.wait("y", "dT1", rec.val("dT1"))
            dma("y", lambda g, g_idx=g_idx: g.dma_start(
                out=oag_i[g_idx * 128:(g_idx + 1) * 128, :], in_=otg[:, :]))
            st["otg_busy"] = rec.val("dS")
            rec.wait("p", "dS", st["otg_busy"])
            rec.emit("p", lambda g, g_idx=g_idx: g.collective_compute(
                "AllGather", ALU.bypass, replica_groups=[list(range(NC))],
                ins=[oag_i[g_idx * 128:(g_idx + 1) * 128, :].opt()],
                outs=[oag_o[g_idx * NC * 128:(g_idx + 1) * NC * 128, :].opt()]), "CC")
            st[("cc_o", g_idx)] = rec.val("CC")
        cc_o = rec.val("CC")

        if upto == 3:
            rec.wait("p", "CC", rec.val("CC"))
            dummy_out()
            return nc
        phase_barrier()

        otall = carve(0, [128, NTO * H * 128], BF16)
        wrow = [carve(64, [128, H * 512], BF16), carve(64 + H, [128, H * 512], BF16)]
        attst = [carve(64 + 2 * H, [128, 512], F32), carve(66 + 2 * H, [128, 512], F32)]
        p4 = PB[0:min(NTO, 8)]

        OGW = NTO * NC * 128
        for gg in range(GS):
            rec.wait("p", "CC", st[("cc_o", gg)])
            dma("p", lambda g, gg=gg: g.dma_start(
                out=oag_l[gg * NC * 128:(gg + 1) * NC * 128, :],
                in_=oag_o[gg * NC * 128:(gg + 1) * NC * 128, :][:, ds(dyn["oo"], TOK)]))
        olv = rec.val("dG")
        rec.wait("p", "dG", olv)
        for gg in range(GS):
            for jj in range(NC):
                dma("p", lambda g, gg=gg, jj=jj: g.dma_start(
                    out=otall[:, gg * OGW + jj * NTO * 128: gg * OGW + (jj + 1) * NTO * 128],
                    in_=oag_l[gg * NC * 128 + jj * 128: gg * NC * 128 + (jj + 1) * 128, :]))
        otv = rec.val("dG")

        for dc in range(DC):
            wb = dc % 2
            wsem = "dW%d" % wb
            rec.wait("p", "T", st.get(("wrow_busy", wb), 0))
            dma("p", lambda g, dc=dc, wb=wb: g.dma_start(
                out=wrow[wb][:, :],
                in_=wot[:, dc * H * 512:(dc + 1) * H * 512]), wsem)
            dwv = rec.val(wsem)
            for rg in range(H):
                for t in range(NTO):
                    rec.wait("t", wsem, dwv)
                    rec.wait("t", "dG", otv)
                    if rg == 0:
                        rec.wait("t", "S" if t % 2 == 0 else "V", st.get(("p4free", t % 8), 0))
                    oco = (rg % GS) * OGW + (rg // GS) * NTO * 128 + t * 128
                    rec.emit("t", lambda tt, t=t, rg=rg, wb=wb, oco=oco: tt.matmul(
                        p4[t % 8][:, :], otall[:, oco: oco + 128],
                        wrow[wb][:, rg * 512:(rg + 1) * 512], start=(rg == 0), stop=(rg == H - 1)), "T")
            st[("wrow_busy", wb)] = rec.val("T")
            twv = rec.val("T")
            for t in range(NTO):
                ap_ = t % 2
                if ap_ == 0:
                    rec.wait("s", "T", twv)
                    rec.wait("s", "dX0", st.get(("attst_busy", 0), 0))
                    rec.emit("s", lambda s, t=t: s.copy(attst[0][:, :], p4[t % 8][:, :]), "S")
                    sv_ = rec.val("S")
                    rec.wait("y", "S", sv_)
                else:
                    rec.wait("v", "T", twv)
                    rec.wait("v", "dX1", st.get(("attst_busy", 1), 0))
                    rec.emit("v", lambda v, t=t: v.tensor_copy(attst[1][:, :], p4[t % 8][:, :]), "V")
                    sv_ = rec.val("V")
                    rec.wait("y", "V", sv_)
                st[("p4free", t % 8)] = sv_
                dma("y", lambda g, t=t, dc=dc, ap_=ap_: g.dma_start(
                    out=attd[t * 128:(t + 1) * 128, dc * 512:(dc + 1) * 512], in_=attst[ap_][:, :]), "dX%d" % ap_)
                st[("attst_busy", ap_)] = rec.val("dX%d" % ap_)
        if upto == 4:
            dummy_out()
            return nc
        p4dma = rec.val("dS")
        phase_barrier()

        x5 = [carve(0, [128, DM], F32), carve(16, [128, DM], F32)]
        at5 = [carve(32, [128, DM], F32), carve(48, [128, DM], F32)]
        x2b = [carve(64, [128, DM], F32), carve(80, [128, DM], F32)]
        ffb = carve(96, [128, DM], F32)
        kfb = carve(112, [128, RC * DM], BF16)
        _o4 = 112 + RC * DM * 2 // 1024
        s8b = carve(_o4, [128, RDIM], F32)
        _o5 = _o4 + (RDIM * 4 + 1023) // 1024 + 1
        s8c = carve(_o5, [128, 2 * RDIM], F32)
        _o6 = _o5 + (2 * RDIM * 4 + 1023) // 1024 + 1
        zbf = [carve(_o6, [128, RDIM], BF16),
               carve(_o6 + (RDIM * 2 + 1023) // 1024 + 1, [128, RDIM], BF16)]
        _o7 = _o6 + 2 * ((RDIM * 2 + 1023) // 1024 + 1)
        zT = [carve(_o7, [128, RC * 128], BF16),
              carve(_o7 + (RC * 128 * 2 + 1023) // 1024 + 1, [128, RC * 128], BF16)]
        outb = x5
        p5 = PB[0:2]

        dma("p", lambda g: g.dma_start(out=kfb[:, :], in_=kfh))
        kfv = rec.val("dG")

        def p5_stageA(t):
            b = t % 2
            rec.wait("y", "dS", p4dma)
            rec.wait("y", "V", st.get(("p5free", b), 0))
            rec.wait("y", "S", st.get(("p5freeS", b), 0))
            rec.wait("y", "dX%d" % b, st.get(("outb_busy", b), 0))
            xsem = "dX%d" % b
            dma("y", lambda g, t=t, b=b: g.dma_start(out=x5[b][:, :], in_=xa[t * 128:(t + 1) * 128, :]), xsem)
            dma("y", lambda g, t=t, b=b: g.dma_start(out=at5[b][:, :], in_=attd[t * 128:(t + 1) * 128, :]), xsem)
            dv = rec.val(xsem)
            rec.wait("v", xsem, dv)
            rec.wait("s", xsem, dv)
            srstd = ln_stats(lambda b=b: at5[b], b)
            rec.wait("v", "S", srstd)
            rec.emit("v", lambda v, b=b: v.tensor_scalar(at5[b][:, :], at5[b][:, :], ln_mean[b][:, :], ln_rstd[b][:, :], ALU.subtract, ALU.mult), "V")
            rec.emit("v", lambda v, b=b: v.tensor_tensor(x2b[b][:, :], x5[b][:, :], at5[b][:, :], ALU.add), "V")
            vx2 = rec.val("V")
            st[("lnV", b)] = vx2
            rec.wait("s", "V", vx2)
            srstd3 = ln_stats(lambda b=b: x2b[b], b)
            rec.wait("v", "S", srstd3)
            rec.emit("v", lambda v, b=b: v.tensor_scalar(ln_t1[b][:, :], ln_mean[b][:, :], 8.0, 0.0, ALU.mult, ALU.add), "V")
            R_ = RDIM
            rec.wait("p", "V", vx2)
            rec.wait("p", "V", st.get("s8bV", 0))
            rec.emit("p", lambda g, b=b: g.tensor_tensor(s8c[:, 0:R_], x2b[b][:, 0:R_], x2b[b][:, R_:2 * R_], ALU.add), "P")
            rec.emit("p", lambda g, b=b: g.tensor_tensor(s8c[:, R_:2 * R_], x2b[b][:, 2 * R_:3 * R_], x2b[b][:, 3 * R_:4 * R_], ALU.add), "P")
            rec.emit("p", lambda g, b=b: g.tensor_tensor(s8b[:, :], x2b[b][:, 4 * R_:5 * R_], x2b[b][:, 5 * R_:6 * R_], ALU.add), "P")
            rec.emit("p", lambda g, b=b: g.tensor_tensor(s8b[:, :], s8b[:, :], x2b[b][:, 6 * R_:7 * R_], ALU.add), "P")
            rec.emit("p", lambda g, b=b: g.tensor_tensor(s8b[:, :], s8b[:, :], x2b[b][:, 7 * R_:8 * R_], ALU.add), "P")
            rec.emit("p", lambda g: g.tensor_tensor(s8c[:, 0:R_], s8c[:, 0:R_], s8c[:, R_:2 * R_], ALU.add), "P")
            rec.emit("p", lambda g: g.tensor_tensor(s8b[:, :], s8b[:, :], s8c[:, 0:R_], ALU.add), "P")
            pv8 = rec.val("P")
            rec.wait("v", "P", pv8)
            rec.wait("v", "dX0", rec.val("dX0"))
            rec.wait("v", "dX1", rec.val("dX1"))
            rec.emit("v", lambda v, b=b: v.tensor_scalar(zbf[b][:, :], s8b[:, :], ln_t1[b][:, :], ln_rstd[b][:, :], ALU.subtract, ALU.mult), "V")
            vz = rec.val("V")
            st[("lnV", b)] = vz
            st["s8bV"] = vz
            rec.wait("y", "V", vz)
            rec.wait("y", "T", st.get(("zTfree", b), 0))
            for r in range(RC):
                slot = r % 2
                rec.wait("y", "dT%d" % slot, st.get(("z_rd", slot), 0))
                dma("y", lambda g, r=r, slot=slot, b=b: g.dma_start(
                    out=oscr[slot * 128:(slot + 1) * 128, :], in_=zbf[b][:, r * 128:(r + 1) * 128]), "dX%d" % slot)
                rec.wait("y", "dX%d" % slot, rec.val("dX%d" % slot))
                dma("y", lambda g, r=r, slot=slot, b=b: g.dma_start(
                    out=zT[b][:, r * 128:(r + 1) * 128],
                    in_=oscr[slot * 128:(slot + 1) * 128, :], transpose=True), "dT%d" % slot)
                st[("z_rd", slot)] = rec.val("dT%d" % slot)
            st[("vzT", b)] = [rec.val("dT0"), rec.val("dT1")]

        def p5_stageB(t):
            b = t % 2
            vzT_d = st[("vzT", b)]
            rec.wait("t", "dG", kfv)
            for dc in range(DC):
                pp = p5[dc % 2]
                rec.wait("t", "dT0", vzT_d[0])
                rec.wait("t", "dT1", vzT_d[1])
                rec.wait("t", "S", st.get(("ffS", dc % 2), 0))
                for r in range(RC):
                    rec.emit("t", lambda tt, pp=pp, r=r, dc=dc, b=b: tt.matmul(
                        pp[:, :], zT[b][:, r * 128:(r + 1) * 128],
                        kfb[:, r * DM + dc * 512: r * DM + (dc + 1) * 512],
                        start=(r == 0), stop=(r == RC - 1)), "T")
                tmm = rec.val("T")
                rec.wait("s", "T", tmm)
                rec.wait("s", "V", st.get("ffb_busyV", 0))
                rec.emit("s", lambda s, pp=pp, dc=dc: s.copy(ffb[:, dc * 512:(dc + 1) * 512], pp[:, :]), "S")
                st[("ffS", dc % 2)] = rec.val("S")
            st[("zTfree", b)] = rec.val("T")
            rec.wait("v", "S", st[("ffS", 0)])
            rec.wait("v", "S", st[("ffS", 1)])
            srstd4 = ln_stats(lambda: ffb, b)
            rec.wait("v", "S", srstd4)
            rec.emit("v", lambda v, b=b: v.tensor_scalar(ffb[:, :], ffb[:, :], ln_mean[b][:, :], ln_rstd[b][:, :], ALU.subtract, ALU.mult), "V")
            st[("lnV", b)] = rec.val("V")
            st[("p5freeS", b)] = rec.val("S")
            rec.wait("v", "dX%d" % b, st.get(("outb_busy", b), 0))
            rec.emit("v", lambda v, b=b: v.tensor_tensor(outb[b][:, :], ffb[:, :], x2b[b][:, :], ALU.add), "V")
            vout = rec.val("V")
            st["ffb_busyV"] = vout
            st[("p5free", b)] = vout
            rec.wait("y", "V", vout)
            dma("y", lambda g, t=t, b=b: g.dma_start(out=yout[t * 128:(t + 1) * 128, :], in_=outb[b][:, :]), "dX%d" % b)
            st[("outb_busy", b)] = rec.val("dX%d" % b)

        for t in range(NTO):
            p5_stageA(t)
            if t >= 1:
                p5_stageB(t - 1)
        p5_stageB(NTO - 1)

        for sem in ALLSEMS:
            rec.wait("y", sem, rec.val(sem))
        rec.wait("p", "dG", rec.val("dG"))
        rec.wait("p", "CC", rec.val("CC"))

        block = e(nc.Block())
        block.gpsimd(lambda g: _replay(rec, "p", g, sems))
        block.sync(lambda y: _replay(rec, "y", y, sems))
        block.tensor(lambda t: _replay(rec, "t", t, sems))
        block.scalar(lambda s: _replay(rec, "s", s, sems))
        block.vector(lambda v: _replay(rec, "v", v, sems))
    return nc


def _bf16(a):
    return np.asarray(a, np.float32).astype(ml_dtypes.bfloat16)


def build_core_inputs(cfg, c, x_full, Wq, Wk, Wv, Wo, qnw, knw, ffk, shared):
    TOK = cfg.TOK
    d = dict(shared)
    d["x"] = np.ascontiguousarray(x_full[c * TOK:(c + 1) * TOK, :], dtype=np.float32)
    DMC_ = cfg.DM // 128
    d["xt"] = _bf16(np.ascontiguousarray(
        d["x"].T.reshape(DMC_, 128, TOK).transpose(1, 0, 2).reshape(128, DMC_ * TOK)))

    heads = [8 * g + c for g in range(GS)]
    slopes = [slope_of(h) for h in heads]
    jobs = make_jobs(cfg)

    lhs3 = np.zeros((3, GS * 128), np.float32)
    for g in range(GS):
        a = np.float32(SQD * slopes[g] * 128.0)
        a_hi = np.float32(ml_dtypes.bfloat16(a))
        a_lo = np.float32(ml_dtypes.bfloat16(np.float32(a - a_hi)))
        lhs3[0, g * 128:(g + 1) * 128] = 1.0
        lhs3[1, g * 128:(g + 1) * 128] = a_hi
        lhs3[2, g * 128:(g + 1) * 128] = a_lo
    d["lhs3"] = _bf16(lhs3)

    qi = np.arange(128, dtype=np.float32)
    qib = np.zeros((1, GS * 512), np.float32)
    for g in range(GS):
        qib[0, g * 512:(g + 1) * 512] = np.tile(-SQD * slopes[g] * qi, 4)
    d["qib"] = qib.astype(np.float32)

    p = np.arange(128, dtype=np.float32)
    bt = np.zeros((128, max(len(jobs), 1)), np.float32)
    for ji, (g, T, ci) in enumerate(jobs):
        sl = slopes[g]
        bt[:, ji] = sl * p + sl * (512.0 * ci - 128.0 * T) + SHIFT
    d["biastab"] = bt

    dyn = np.zeros((1, 16), np.int32)
    for g in range(GS):
        hq = 8 * g + c
        dyn[0, g] = (hq // 4) * NC * 512 + (hq % 4) * 128
        hk = 2 * g + c // 4
        dyn[0, 4 + g] = (hk // 4) * NC * 512 + (hk % 4) * 128
        dyn[0, 9 + g] = hk * 128
    dyn[0, 8] = c * TOK
    d["dynoff"] = dyn
    return d


def build_shared_inputs(cfg, Wq, Wk, Wv, Wo, qnw, knw, ffk):
    DM = cfg.DM
    d = {}
    def _chunked(wt):
        DMC = DM // 128
        ncc = wt.shape[1] // 512
        return np.ascontiguousarray(
            wt.reshape(DMC, 128, ncc, 512).transpose(1, 2, 0, 3).reshape(128, ncc * DMC * 512))

    d["wqt"] = _bf16(_chunked(np.asarray(Wq, np.float32).T))
    d["wkt"] = _bf16(_chunked(np.asarray(Wk, np.float32).T))
    d["wvt"] = _bf16(_chunked(np.asarray(Wv, np.float32).T))
    WoT = np.ascontiguousarray(Wo.T, dtype=np.float32).reshape(H, HD, DM)
    perm = [r + 8 * g for r in range(NC) for g in range(GS)]
    wo2 = WoT[perm].reshape(H, 128, DM // 512, 512)
    d["wot"] = _bf16(np.ascontiguousarray(
        wo2.transpose(1, 2, 0, 3).reshape(128, (DM // 512) * H * 512)))
    Kf = np.asarray(ffk).reshape(cfg.RDIM, DM)
    d["kfh"] = np.ascontiguousarray(
        Kf.reshape(cfg.RC, 128, DM).transpose(1, 0, 2).reshape(128, cfg.RC * DM)).astype(np.int8)
    d["qnw"] = np.asarray(qnw, np.float32).reshape(1, HD)
    d["knw"] = np.asarray(knw, np.float32).reshape(1, HD)
    d["onesf"] = np.ones((1, 128), np.float32)
    cvh = np.zeros((128, 6), np.float32)
    cvh[:, 0] = EPS_LN; cvh[:, 1] = EPS_RMS; cvh[:, 2] = -0.5
    cvh[:, 3] = 1.0 / HD; cvh[:, 4] = INV; cvh[:, 5] = 1.0 / DM
    d["constv"] = cvh
    d["identb"] = np.eye(128, dtype=ml_dtypes.bfloat16)
    d["identf"] = np.eye(128, dtype=np.float32)
    jv = np.zeros((2, 512), np.float32)
    for j in range(4):
        jv[:, j * 128:(j + 1) * 128] = j
    d["jvec2"] = _bf16(jv)
    sel = np.tile(np.eye(128, dtype=np.float32), (1, 4))
    d["selq"] = _bf16(sel)
    masks = np.zeros((128, 4 * 512), np.uint8)
    p = np.arange(128, dtype=np.float32)
    qi = np.arange(128, dtype=np.float32)
    for ph in range(4):
        for j in range(4):
            srel = 128.0 * j + p[:, None]
            qrel = 128.0 * ph + qi[None, :]
            masks[:, ph * 512 + j * 128: ph * 512 + (j + 1) * 128] = (srel > qrel).astype(np.uint8)
    d["maskst"] = masks
    return d


_CFG = CFG()


def kernel(L0_attn_x, L0_attn_Wq, L0_attn_Wk, L0_attn_Wv, L0_attn_Wo,
           L0_attn_q_norm_w, L0_attn_k_norm_w, L0_ffn_kernel):
    cfg = _CFG
    x = np.ascontiguousarray(np.asarray(L0_attn_x, np.float32).reshape(cfg.S, cfg.DM))
    Wq = np.asarray(L0_attn_Wq, np.float32)
    Wk = np.asarray(L0_attn_Wk, np.float32)
    Wv = np.asarray(L0_attn_Wv, np.float32)
    Wo = np.asarray(L0_attn_Wo, np.float32)
    shared = build_shared_inputs(cfg, Wq, Wk, Wv, Wo, L0_attn_q_norm_w,
                                 L0_attn_k_norm_w, L0_ffn_kernel)
    nc = build_program(cfg)
    in_maps = [build_core_inputs(cfg, c, x, Wq, Wk, Wv, Wo, L0_attn_q_norm_w,
                                 L0_attn_k_norm_w, L0_ffn_kernel, shared)
               for c in range(NC)]
    res = run_bass_kernel_spmd(nc, in_maps, core_ids=list(range(NC)))
    out = np.concatenate([res.results[c]["yout"] for c in range(NC)], axis=0)
    return out.reshape(1, cfg.S, cfg.DM)
